# version 06
# baseline (speedup 1.0000x reference)
import numpy as np
import ml_dtypes
from contextlib import ExitStack

import concourse.bass as bass
import concourse.mybir as mybir
from concourse.bass_utils import run_bass_kernel_spmd

F32 = mybir.dt.float32
BF16 = mybir.dt.bfloat16
OP = mybir.AluOpType
AF = mybir.ActivationFunctionType
AX = mybir.AxisListType.X

S, D, NC, OWN, HALO = 8192, 4096, 8, 1024, 1536
H, DH, NF = 32, 128, 32
EPS = 1e-5
WINS = [(0, 0, 256), (256, 0, 512), (512, 256, 512), (768, 512, 512), (1024, 768, 256)]


class Sem:
    def __init__(self, ctx, nc, name):
        self.h = ctx.enter_context(nc.semaphore(name))
        self.v = 0

    def inc(self, instr, n=1):
        instr.then_inc(self.h, n)
        self.v += n
        return self.v


def build_nc():
    nc = bass.Bass("TRN2", target_bir_lowering=False)

    xT = nc.dram_tensor("xT", [D, HALO], F32, kind="ExternalInput")
    wq = nc.dram_tensor("wq", [32, 128, 4096], F32, kind="ExternalInput")
    wo = nc.dram_tensor("wo", [32, 128, 4096], F32, kind="ExternalInput")
    wb = nc.dram_tensor("wb", [32, 128, 4096], F32, kind="ExternalInput")
    wk = nc.dram_tensor("wk", [128, 4096], F32, kind="ExternalInput")
    wv = nc.dram_tensor("wv", [128, 4096], F32, kind="ExternalInput")
    wr = nc.dram_tensor("wr", [128, 256], F32, kind="ExternalInput")
    apk = nc.dram_tensor("apk", [128, 4096], F32, kind="ExternalInput")
    bpk = nc.dram_tensor("bpk", [128, 4096], F32, kind="ExternalInput")
    bb = nc.dram_tensor("bb", [128, 32], F32, kind="ExternalInput")
    br = nc.dram_tensor("br", [8, 1], F32, kind="ExternalInput")
    cosq = nc.dram_tensor("cosq", [128, OWN], F32, kind="ExternalInput")
    sinq = nc.dram_tensor("sinq", [128, OWN], F32, kind="ExternalInput")
    cosk = nc.dram_tensor("cosk", [128, HALO], F32, kind="ExternalInput")
    sink = nc.dram_tensor("sink", [128, HALO], F32, kind="ExternalInput")
    maskk = nc.dram_tensor("maskk", [128, 12], F32, kind="ExternalInput")
    alpha = nc.dram_tensor("alpha", [128, OWN], F32, kind="ExternalInput")
    identb = nc.dram_tensor("identb", [128, 128], BF16, kind="ExternalInput")
    idf8 = nc.dram_tensor("idf8", [8, 8], F32, kind="ExternalInput")
    ptr = nc.dram_tensor("ptr", [128, 128], BF16, kind="ExternalInput")
    eexp = nc.dram_tensor("eexp", [8, 128], BF16, kind="ExternalInput")
    x1scr = nc.dram_tensor("x1scr", [D, OWN], F32)
    outT = nc.dram_tensor("outT", [D, OWN], F32, kind="ExternalOutput")

    ctx = ExitStack()
    with ctx:
        sb = lambda n, s, d: ctx.enter_context(nc.sbuf_tensor(n, s, d))
        bigA = sb("bigA", [128, 32 * 1024], BF16)
        bigB = sb("bigB", [128, 32 * 1024], BF16)
        sgbuf = sb("sgbuf", [128, 4096], BF16)
        wbf = sb("wbf", [128, 3 * 2048], BF16)
        fbA = sb("fbA", [128, HALO], F32)
        fbB = sb("fbB", [128, HALO], F32)
        fbC = sb("fbC", [128, OWN], F32)
        fbD = sb("fbD", [128, OWN], F32)
        kr_s = sb("kr_s", [128, HALO], BF16)
        ku_s = sb("ku_s", [128, HALO], BF16)
        vtm_s = sb("vtm_s", [128, HALO], BF16)
        tmp = sb("tmp", [128, HALO], F32)
        ctxn = [sb("ctxn0", [128, 512], F32), sb("ctxn1", [128, 512], F32)]
        u_bf = sb("u_bf", [128, OWN], BF16)
        wr_b = sb("wr_b", [128, 256], BF16)
        tmp2g = sb("tmp2g", [128, 512], F32)
        a_sb = sb("a_sb", [128, 1024], F32)
        b_sb = sb("b_sb", [128, 1024], F32)
        rd_hl = [sb("rd_h0", [1, 512], BF16), sb("rd_h1", [1, 512], BF16)]
        z_sb = sb("z_sb", [128, 64], F32)
        z8 = sb("z8", [128, 8], F32)
        mk_sb = sb("mk_sb", [128, 8], F32)
        col_s = sb("col_s", [128, 1], F32)
        col_c = sb("col_c", [128, 1], F32)
        col_t = sb("col_t", [128, 1], F32)
        w_tm = sb("w_tm", [128, 64], BF16)
        w_sb = sb("w_sb", [8, 1024], BF16)
        identb_s = sb("identb_s", [128, 128], BF16)
        idf8_s = sb("idf8_s", [8, 8], F32)
        ptr_s = sb("ptr_s", [128, 128], BF16)
        eexp_s = sb("eexp_s", [8, 128], BF16)
        ones_c = sb("ones_c", [128, 1], BF16)
        ones_r = sb("ones_r", [1, 128], BF16)
        maskk_s = sb("maskk_s", [128, 12], F32)
        bb_s = sb("bb_s", [128, 32], F32)
        br_s = sb("br_s", [8, 1], F32)
        eps_s = sb("eps_s", [1, 1], F32)
        lg_sb = fbA
        assert nc.sbuf_bytes_remaining > 512, nc.sbuf_bytes_remaining

        EDG = 0
        WKO = 16 * 1024
        WVO = 20 * 1024

        sW = Sem(ctx, nc, "sW")
        sXa = Sem(ctx, nc, "sXa")
        sXb = Sem(ctx, nc, "sXb")
        sG = [Sem(ctx, nc, f"sG{i}") for i in range(3)]
        sS1 = Sem(ctx, nc, "sS1")
        sS2 = Sem(ctx, nc, "sS2")
        sP = Sem(ctx, nc, "sP")
        sT = Sem(ctx, nc, "sT")
        sV = Sem(ctx, nc, "sV")
        sC = Sem(ctx, nc, "sC")

        es1 = ExitStack()
        psum_sum = es1.enter_context(nc.psum_tensor("psum_sum", [128, HALO], F32))
        psum_sq = es1.enter_context(nc.psum_tensor("psum_sq", [128, HALO], F32))

        bT, bC = sT.v, sC.v
        grp_sems = [sG[0], sG[1], sG[2], sXa]
        grp_after = [sem.v + 48 for sem in grp_sems]
        bXB = sXb.v
        sc_after = lambda fs: bC + fs + 1
        tt_after = lambda fs: bT + fs + 1

        with nc.named_scope("p1"), nc.Block("p1") as blk:
            @blk.sync
            def _(sync):
                for dst, src in ((identb_s, identb), (idf8_s, idf8), (ptr_s, ptr),
                                 (eexp_s, eexp), (maskk_s, maskk), (bb_s, bb),
                                 (br_s, br)):
                    sW.inc(sync.dma_start(out=dst[:, :], in_=src[:, :]), 16)

            @blk.vector
            def _(vector):
                vector.memset(ones_c[:, :], 1.0)
                vector.memset(eps_s[:, :], EPS)
                sV.inc(vector.memset(ones_r[:, :], 1.0))
            vONES = sV.v

            @blk.gpsimd
            def _(gpsimd):
                for g in range(4):
                    own_src = bass.AP(xT, g * 1024 * HALO + 256,
                                      [[HALO, 128], [128 * HALO, 8], [1, 1024]])
                    own_dst = bass.AP(bigA, g * 8192,
                                      [[32 * 1024, 128], [1024, 8], [1, 1024]])
                    grp_sems[g].inc(gpsimd.dma_start(out=own_dst, in_=own_src), 16)
                    for side in range(2):
                        edg_src = bass.AP(xT, g * 1024 * HALO + side * 1280,
                                          [[HALO, 128], [128 * HALO, 8], [1, 256]])
                        edg_dst = bass.AP(bigB, EDG + g * 4096 + side * 256,
                                          [[32 * 1024, 128], [512, 8], [1, 256]])
                        grp_sems[g].inc(gpsimd.dma_start(out=edg_dst, in_=edg_src), 16)
                sXb.inc(gpsimd.dma_start(out=wr_b[:, :], in_=wr[:, :]), 16)
                sXb.inc(gpsimd.dma_start(out=bigB[:, WKO:WKO + 4096], in_=wk[:, :]), 16)
                sXb.inc(gpsimd.dma_start(out=bigB[:, WVO:WVO + 4096], in_=wv[:, :]), 16)

            @blk.scalar
            def _(scalar):
                for fs in range(NF):
                    if fs % 8 == 0:
                        scalar.wait_ge(grp_sems[fs // 8].h, grp_after[fs // 8])
                    if fs >= 2:
                        scalar.wait_ge(sT.h, tt_after(fs - 2))
                    so = (fs % 2) * 2048
                    scalar.activation(out=sgbuf[:, so:so + 1024],
                                      in_=bigA[:, fs * 1024:(fs + 1) * 1024], func=AF.Square)
                    ins = scalar.activation(out=sgbuf[:, so + 1024:so + 1536],
                                            in_=bigB[:, EDG + fs * 512:EDG + fs * 512 + 512],
                                            func=AF.Square)
                    assert sC.inc(ins) == sc_after(fs)

            @blk.tensor
            def _(tensor):
                for fs in range(NF):
                    st_, sp_ = fs == 0, fs == NF - 1
                    if fs == 0:
                        tensor.wait_ge(sV.h, vONES)
                    if fs % 8 == 0:
                        tensor.wait_ge(grp_sems[fs // 8].h, grp_after[fs // 8])
                    tensor.wait_ge(sC.h, sc_after(fs))
                    so = (fs % 2) * 2048
                    a0 = fs * 1024
                    e0 = EDG + fs * 512
                    tensor.matmul(psum_sum[0:1, 0:512], ones_c[:, :], bigA[:, a0:a0 + 512], start=st_, stop=sp_)
                    tensor.matmul(psum_sum[0:1, 512:1024], ones_c[:, :], bigA[:, a0 + 512:a0 + 1024], start=st_, stop=sp_)
                    tensor.matmul(psum_sum[0:1, 1024:1280], ones_c[:, :], bigB[:, e0:e0 + 256], start=st_, stop=sp_)
                    tensor.matmul(psum_sum[0:1, 1280:1536], ones_c[:, :], bigB[:, e0 + 256:e0 + 512], start=st_, stop=sp_)
                    tensor.matmul(psum_sq[0:1, 0:512], ones_c[:, :], sgbuf[:, so:so + 512], start=st_, stop=sp_)
                    tensor.matmul(psum_sq[0:1, 512:1024], ones_c[:, :], sgbuf[:, so + 512:so + 1024], start=st_, stop=sp_)
                    tensor.matmul(psum_sq[0:1, 1024:1280], ones_c[:, :], sgbuf[:, so + 1024:so + 1280], start=st_, stop=sp_)
                    ins = tensor.matmul(psum_sq[0:1, 1280:1536], ones_c[:, :], sgbuf[:, so + 1280:so + 1536], start=st_, stop=sp_)
                    assert sT.inc(ins) == tt_after(fs)

        def ln_coef_block(name, psA, psB, n, m_row, w_row, hi_row, lo_row):
            with nc.named_scope(name), nc.Block(name) as blk2:
                @blk2.vector
                def _(vector):
                    vector.tensor_scalar_mul(m_row, psA[0:1, 0:n], 1.0 / D)
                    vector.scalar_tensor_tensor(w_row, m_row, -1.0, m_row, OP.mult, OP.mult)
                    ins = vector.scalar_tensor_tensor(w_row, psB[0:1, 0:n], 1.0 / D, w_row, OP.mult, OP.add)
                    sV.inc(ins)
                vS = sV.v

                @blk2.scalar
                def _(scalar):
                    scalar.wait_ge(sV.h, vS)
                    sC.inc(scalar.activation(out=w_row, in_=w_row, func=AF.Sqrt, bias=eps_s[0:1, 0:1]))
                cS = sC.v

                @blk2.vector
                def _(vector):
                    vector.wait_ge(sC.h, cS)
                    vector.reciprocal(w_row, w_row)
                    vector.tensor_mul(m_row, m_row, w_row)
                    vector.tensor_scalar_mul(m_row, m_row, -1.0)
                    vector.tensor_copy(out=hi_row, in_=w_row)
                    vector.tensor_sub(lo_row, w_row, hi_row)
                    sV.inc(vector.engine_nop())

        m1 = fbA[0:1, 0:HALO]
        w1 = fbB[0:1, 0:HALO]
        hi1 = sgbuf[0:1, 0:HALO]
        lo1 = sgbuf[0:1, 2048:2048 + HALO]
        ln_coef_block("p2a", psum_sum, psum_sq, HALO, m1, w1, hi1, lo1)
        es1.close()


        def bcast_block(name, n, m_row, hi_row, lo_row):
            es2o = ExitStack()
            psA = es2o.enter_context(nc.psum_tensor(name + "_pa", [128, 1024], F32))
            psB = es2o.enter_context(nc.psum_tensor(name + "_pb", [128, 1024], F32))
            if n > 1024:
                edge_ps = (es2o.enter_context(nc.psum_tensor(name + "_pae", [128, 512], F32)),
                           es2o.enter_context(nc.psum_tensor(name + "_pbe", [128, 512], F32)))
            with nc.named_scope(name), nc.Block(name) as blk2:
                @blk2.tensor
                def _(tensor):
                    for c in range(0, n, 512):
                        w = min(512, n - c)
                        dst = psA[:, c:c + w] if c < 1024 else edge_ps[0][:, c - 1024:c - 1024 + w]
                        tensor.matmul(dst, ones_r[:, :], hi_row[0:1, c:c + w], start=True, stop=False)
                        ins = tensor.matmul(dst, ones_r[:, :], lo_row[0:1, c:c + w], start=False, stop=True)
                    sT.inc(ins)
                tS = sT.v

                @blk2.vector
                def _(vector):
                    vector.wait_ge(sT.h, tS)
                    vector.tensor_copy(out=hi_row, in_=m_row)
                    vector.tensor_sub(lo_row, m_row, hi_row)
                    sV.inc(vector.engine_nop())
                vS = sV.v

                @blk2.tensor
                def _(tensor):
                    tensor.wait_ge(sV.h, vS)
                    for c in range(0, n, 512):
                        w = min(512, n - c)
                        dst = psB[:, c:c + w] if c < 1024 else edge_ps[1][:, c - 1024:c - 1024 + w]
                        tensor.matmul(dst, ones_r[:, :], hi_row[0:1, c:c + w], start=True, stop=False)
                        ins = tensor.matmul(dst, ones_r[:, :], lo_row[0:1, c:c + w], start=False, stop=True)
                    sT.inc(ins)
                tS2 = sT.v

                @blk2.scalar
                def _(scalar):
                    scalar.wait_ge(sT.h, tS2)
                    scalar.copy(out=a_sb[:, :], in_=psA[:, :])
                    ins = scalar.copy(out=b_sb[:, :], in_=psB[:, :])
                    if n > 1024:
                        scalar.copy(out=ctxn[0][:, :], in_=edge_ps[0][:, :])
                        ins = scalar.copy(out=ctxn[1][:, :], in_=edge_ps[1][:, :])
                    sC.inc(ins)
            es2o.close()

        bcast_block("p2b", HALO, m1, hi1, lo1)

        GPSET = set(fs for fs in range(NF) if fs % 3 == 0)
        bV3, bP3 = sV.v, sP.v
        vnorm = {}
        gnorm = {}
        _vv = bV3
        for fs in range(NF):
            _vv += 1; vnorm[fs] = _vv
        _gg = bP3
        for fs in sorted(GPSET):
            _gg += 1; gnorm[fs] = _gg

        es3v = ExitStack()
        psum_v = es3v.enter_context(nc.psum_tensor("psum_v", [128, HALO], F32))
        es3 = ExitStack()
        psum_k = es3.enter_context(nc.psum_tensor("psum_k", [128, HALO], F32))

        with nc.named_scope("p3kv"), nc.Block("p3kv") as blk:
            @blk.sync
            def _(sync):
                sW.inc(sync.dma_start(out=fbA[:, :], in_=cosk[:, :]), 16)
                sW.inc(sync.dma_start(out=fbB[:, :], in_=sink[:, :]), 16)
            wCK = sW.v

            @blk.vector
            def _(vector):
                for fs in range(NF):
                    a0 = fs * 1024
                    e0 = EDG + fs * 512
                    if fs not in GPSET:
                        vector.tensor_mul(tmp[:, 0:1024], bigA[:, a0:a0 + 1024], a_sb[:, :])
                        vector.tensor_add(bigA[:, a0:a0 + 1024], tmp[:, 0:1024], b_sb[:, :])
                    vector.tensor_mul(tmp[:, 1024:1536], bigB[:, e0:e0 + 512], ctxn[0][:, :])
                    ins = vector.tensor_add(bigB[:, e0:e0 + 512], tmp[:, 1024:1536], ctxn[1][:, :])
                    assert sV.inc(ins) == vnorm[fs]

            @blk.gpsimd
            def _(gpsimd):
                for fs in sorted(GPSET):
                    a0 = fs * 1024
                    for hh in range(2):
                        o = a0 + hh * 512
                        gpsimd.tensor_mul(tmp2g[:, :], bigA[:, o:o + 512], a_sb[:, hh * 512:(hh + 1) * 512])
                        ins = gpsimd.tensor_add(bigA[:, o:o + 512], tmp2g[:, :], b_sb[:, hh * 512:(hh + 1) * 512])
                    assert sP.inc(ins) == gnorm[fs]

            @blk.tensor
            def _(tensor):
                tensor.wait_ge(sXb.h, bXB + 48)
                for fs in range(NF):
                    st_, sp_ = fs == 0, fs == NF - 1
                    a0 = fs * 1024
                    e0 = EDG + fs * 512
                    tensor.wait_ge(sV.h, vnorm[fs])
                    if fs in GPSET:
                        tensor.wait_ge(sP.h, gnorm[fs])
                    for ps, woff in ((psum_k, WKO), (psum_v, WVO)):
                        lhs = bigB[:, woff + fs * 128:woff + (fs + 1) * 128]
                        ps_ = ps
                        tensor.matmul(ps_[:, 0:512], lhs, bigA[:, a0:a0 + 512], start=st_, stop=sp_)
                        tensor.matmul(ps_[:, 512:1024], lhs, bigA[:, a0 + 512:a0 + 1024], start=st_, stop=sp_)
                        tensor.matmul(ps_[:, 1024:1280], lhs, bigB[:, e0:e0 + 256], start=st_, stop=sp_)
                        ins = tensor.matmul(ps_[:, 1280:1536], lhs, bigB[:, e0 + 256:e0 + 512], start=st_, stop=sp_)
                sT.inc(ins)
            tKV = sT.v

            @blk.scalar
            def _(scalar):
                scalar.wait_ge(sT.h, tKV)
                scalar.copy(out=ku_s[:, 256:1280], in_=psum_k[:, 0:1024])
                scalar.copy(out=ku_s[:, 0:256], in_=psum_k[:, 1024:1280])
                sC.inc(scalar.copy(out=ku_s[:, 1280:1536], in_=psum_k[:, 1280:1536]))
            cK = sC.v
        es3.close()

        es3r = ExitStack()
        psum_rot = es3r.enter_context(nc.psum_tensor("psum_rot", [128, HALO], F32))
        psum_t = [es3r.enter_context(nc.psum_tensor(f"psum_t{i}", [128, 128], BF16)) for i in range(2)]

        wALL3 = sW.v

        with nc.named_scope("p3r"), nc.Block("p3r") as blk:
            @blk.tensor
            def _(tensor):
                tensor.wait_ge(sW.h, wALL3)
                tensor.wait_ge(sC.h, cK)
                for c in range(0, HALO, 512):
                    ins = tensor.matmul(psum_rot[:, c:c + 512], ptr_s[:, :], ku_s[:, c:c + 512], start=True, stop=True)
                sT.inc(ins)
            tROT = sT.v

            @blk.vector
            def _(vector):
                vector.wait_ge(sW.h, wCK)
                vector.tensor_mul(tmp[:, :], ku_s[:, :], fbA[:, :])
                vector.wait_ge(sT.h, tROT)
                vector.tensor_tensor(psum_rot[:, :], psum_rot[:, :], fbB[:, :], op=OP.mult)
                vector.tensor_add(kr_s[:, :], tmp[:, :], psum_rot[:, :])
                sV.inc(vector.engine_nop())
            vKR = sV.v

            @blk.scalar
            def _(scalar):
                scalar.wait_ge(sV.h, vKR)
                scalar.copy(out=ku_s[:, 256:1280], in_=psum_v[:, 0:1024])
                scalar.copy(out=ku_s[:, 0:256], in_=psum_v[:, 1024:1280])
                sC.inc(scalar.copy(out=ku_s[:, 1280:1536], in_=psum_v[:, 1280:1536]))
            cV = sC.v

            cv_pre = [cV + 1 + j for j in range(12)]
            tv_pre = [tROT + 1 + j for j in range(12)]

            @blk.tensor
            def _(tensor):
                tensor.wait_ge(sC.h, cV)
                for j in range(12):
                    if j >= 2:
                        tensor.wait_ge(sC.h, cv_pre[j - 2])
                    ins = tensor.transpose(psum_t[j % 2][:, :], ku_s[:, j * 128:(j + 1) * 128], identb_s[:, :])
                    assert sT.inc(ins) == tv_pre[j]

            @blk.scalar
            def _(scalar):
                for j in range(12):
                    scalar.wait_ge(sT.h, tv_pre[j])
                    ins = scalar.copy(out=vtm_s[:, j * 128:(j + 1) * 128], in_=psum_t[j % 2][:, :])
                    assert sC.inc(ins) == cv_pre[j]
        es3r.close()
        es3v.close()

        es4 = ExitStack()
        psum_q = [es4.enter_context(nc.psum_tensor(f"psum_q{i}", [128, OWN], F32)) for i in range(2)]
        psum_rq = es4.enter_context(nc.psum_tensor("psum_rq", [128, OWN], F32))

        bT, bC, bV = sT.v, sC.v, sV.v
        t_q = {}
        t_rot = {}
        _tt = bT
        for h in range(H):
            _tt += 1; t_q[h] = _tt
            if h >= 1:
                _tt += 1; t_rot[h - 1] = _tt
        _tt += 1; t_rot[H - 1] = _tt
        c_u = lambda h: bC + h + 1
        v_qr = lambda h: bV + h + 1
        bGs = [s.v for s in sG]
        g_after_p4 = []
        gtmp = list(bGs)
        for j in range(64):
            gtmp[j % 3] += 16
            g_after_p4.append(gtmp[j % 3])

        with nc.named_scope("p4"), nc.Block("p4") as blk:
            @blk.sync
            def _(sync):
                sW.inc(sync.dma_start(out=fbC[:, :], in_=cosq[:, :]), 16)
                sW.inc(sync.dma_start(out=fbD[:, :], in_=sinq[:, :]), 16)
            wCQ = sW.v

            @blk.gpsimd
            def _(gpsimd):
                for j in range(64):
                    if j >= 3:
                        gpsimd.wait_ge(sT.h, t_q[(j - 3) // 2])
                    slot = j % 3
                    oc, hf = j // 2, j % 2
                    ins = gpsimd.dma_start(
                        out=wbf[:, slot * 2048:(slot + 1) * 2048],
                        in_=wq[oc, :, hf * 2048:(hf + 1) * 2048])
                    assert sG[slot].inc(ins, 16) == g_after_p4[j]

            @blk.tensor
            def _(tensor):
                def rot_mms(j):
                    tensor.wait_ge(sC.h, c_u(j))
                    if j >= 1:
                        tensor.wait_ge(sV.h, v_qr(j - 1))
                    tensor.matmul(psum_rq[:, 0:512], ptr_s[:, :], u_bf[:, 0:512], start=True, stop=True)
                    ins = tensor.matmul(psum_rq[:, 512:1024], ptr_s[:, :], u_bf[:, 512:1024], start=True, stop=True)
                    assert sT.inc(ins) == t_rot[j]
                for h in range(H):
                    if h >= 2:
                        tensor.wait_ge(sC.h, c_u(h - 2))
                    pq = psum_q[h % 2]
                    for fs in range(NF):
                        if fs == 0:
                            tensor.wait_ge(sG[(2 * h) % 3].h, g_after_p4[2 * h])
                        if fs == 16:
                            tensor.wait_ge(sG[(2 * h + 1) % 3].h, g_after_p4[2 * h + 1])
                        slot = (2 * h + fs // 16) % 3
                        lhs = wbf[:, slot * 2048 + (fs % 16) * 128:slot * 2048 + (fs % 16 + 1) * 128]
                        st_, sp_ = fs == 0, fs == NF - 1
                        tensor.matmul(pq[:, 0:512], lhs, bigA[:, fs * 1024:fs * 1024 + 512], start=st_, stop=sp_)
                        ins = tensor.matmul(pq[:, 512:1024], lhs, bigA[:, fs * 1024 + 512:fs * 1024 + 1024], start=st_, stop=sp_)
                    assert sT.inc(ins) == t_q[h]
                    if h >= 1:
                        rot_mms(h - 1)
                rot_mms(H - 1)

            @blk.scalar
            def _(scalar):
                for h in range(H):
                    scalar.wait_ge(sT.h, t_q[h])
                    if h >= 1:
                        scalar.wait_ge(sV.h, v_qr(h - 1))
                        scalar.wait_ge(sT.h, t_rot[h - 1])
                    ins = scalar.copy(out=u_bf[:, :], in_=psum_q[h % 2][:, :])
                    assert sC.inc(ins) == c_u(h)

            @blk.vector
            def _(vector):
                for h in range(H):
                    vector.wait_ge(sC.h, c_u(h))
                    if h == 0:
                        vector.wait_ge(sW.h, wCQ)
                    vector.tensor_mul(tmp[:, 0:1024], u_bf[:, :], fbC[:, :])
                    vector.wait_ge(sT.h, t_rot[h])
                    vector.tensor_tensor(psum_rq[:, :], psum_rq[:, :], fbD[:, :], op=OP.mult)
                    ins = vector.tensor_add(bigB[:, h * 1024:(h + 1) * 1024], tmp[:, 0:1024], psum_rq[:, :])
                    assert sV.inc(ins) == v_qr(h)
        es4.close()

        es5 = ExitStack()
        psum_Sb = es5.enter_context(nc.psum_tensor("psum_Sb", [128, 2048], F32))
        psum_S = [psum_Sb[:, kt * 512:(kt + 1) * 512] for kt in range(4)]
        psum_ctx = [es5.enter_context(nc.psum_tensor(f"psum_ctx{i}", [128, 512], F32)) for i in range(2)]
        psum_d = es5.enter_context(nc.psum_tensor("psum_d", [128, 512], F32))
        psum_db = es5.enter_context(nc.psum_tensor("psum_db", [128, 512], F32))

        NWIN = H * 5
        WIN = lambda j: (j // 5,) + WINS[j % 5]
        bT, bC, bV = sT.v, sC.v, sV.v
        t_s = {}
        t_d = {}
        t_ctx = {}
        t_db = {}
        _tt = bT
        for i in range(NWIN + 2):
            if i < NWIN:
                _tt += 1; t_s[i] = _tt
            if 1 <= i <= NWIN:
                _tt += 1; t_d[i - 1] = _tt
                _tt += 1; t_ctx[i - 1] = _tt
            if 2 <= i <= NWIN + 1:
                _tt += 1; t_db[i - 2] = _tt
        c_sig = {}
        c_db = {}
        _cc = bC
        for i in range(NWIN + 2):
            if i < NWIN:
                _cc += 1; c_sig[i] = _cc
            if 2 <= i <= NWIN + 1:
                _cc += 1; c_db[i - 2] = _cc
        v_rd = {}
        v_dc = {}
        _vv = bV
        for i in range(NWIN + 2):
            if 1 <= i <= NWIN:
                _vv += 1; v_rd[i - 1] = _vv
            if 2 <= i <= NWIN + 1:
                _vv += 1; v_dc[i - 2] = _vv
        nblend = {-1: 0}
        _nb = 0
        for j in range(NWIN):
            if j % 5 >= 1:
                _nb += 1
            nblend[j] = _nb
        bP = sP.v

        with nc.named_scope("p5"), nc.Block("p5") as blk:
            @blk.sync
            def _(sync):
                sW.inc(sync.dma_start(out=fbC[:, :], in_=alpha[:, :]), 16)
            wAL = sW.v

            @blk.tensor
            def _(tensor):
                for i in range(NWIN + 2):
                    if i < NWIN:
                        h, ks, q0, ql = WIN(i)
                        qr = bigB[:, h * 1024 + q0:h * 1024 + q0 + ql]
                        if i >= 1:
                            tensor.wait_ge(sC.h, c_sig[i - 1])
                        for kt in range(4):
                            ins = tensor.matmul(psum_S[kt][:, 0:ql], kr_s[:, ks + kt * 128:ks + (kt + 1) * 128],
                                                qr, start=True, stop=True)
                        assert sT.inc(ins) == t_s[i]
                    if 1 <= i <= NWIN:
                        j = i - 1
                        h, ks, q0, ql = WIN(j)
                        sl = (j % 2) * 2048
                        if i == 1:
                            tensor.wait_ge(sC.h, c_sig[0])
                        if j >= 2:
                            tensor.wait_ge(sV.h, v_rd[j - 2])
                        row = (j % 2) * 32
                        for kt in range(4):
                            ins = tensor.matmul(psum_d[row:row + 1, 0:ql], ones_c[:, :],
                                                sgbuf[:, sl + kt * 512:sl + kt * 512 + ql],
                                                start=kt == 0, stop=kt == 3)
                        assert sT.inc(ins) == t_d[j]
                        if j >= 2:
                            tensor.wait_ge(sV.h, v_dc[j - 2])
                        pc = psum_ctx[j % 2]
                        for kt in range(4):
                            ins = tensor.matmul(pc[:, 0:ql], vtm_s[:, ks + kt * 128:ks + (kt + 1) * 128],
                                                sgbuf[:, sl + kt * 512:sl + kt * 512 + ql],
                                                start=kt == 0, stop=kt == 3)
                        assert sT.inc(ins) == t_ctx[j]
                    if 2 <= i <= NWIN + 1:
                        j = i - 2
                        h, ks, q0, ql = WIN(j)
                        tensor.wait_ge(sV.h, v_rd[j])
                        if j >= 1:
                            tensor.wait_ge(sC.h, c_db[j - 1])
                        rh = rd_hl[j % 2]
                        ins = tensor.matmul(psum_db[:, 0:ql], ones_r[:, :], rh[0:1, 0:ql], start=True, stop=True)
                        assert sT.inc(ins) == t_db[j]

            @blk.scalar
            def _(scalar):
                for i in range(NWIN + 2):
                    if i < NWIN:
                        h, ks, q0, ql = WIN(i)
                        sl = (i % 2) * 2048
                        if i == 0:
                            scalar.wait_ge(sW.h, wAL)
                        scalar.wait_ge(sT.h, t_s[i])
                        if i >= 2:
                            scalar.wait_ge(sT.h, t_ctx[i - 2])
                        if i % 5 in (1, 2, 3):
                            ins = scalar.activation(out=sgbuf[:, sl:sl + 2048],
                                                    in_=psum_Sb[:, :], func=AF.Sigmoid)
                        else:
                            for kt in range(4):
                                kc = ks // 128 + kt
                                ins = scalar.activation(out=sgbuf[:, sl + kt * 512:sl + kt * 512 + ql],
                                                        in_=psum_S[kt][:, 0:ql],
                                                        func=AF.Sigmoid, bias=maskk_s[:, kc:kc + 1], scale=1.0)
                        assert sC.inc(ins) == c_sig[i]
                    if 2 <= i <= NWIN + 1:
                        j = i - 2
                        h, ks, q0, ql = WIN(j)
                        scalar.wait_ge(sT.h, t_db[j])
                        if j >= 1:
                            scalar.wait_ge(sV.h, v_dc[j - 1])
                        ins = scalar.copy(out=tmp[:, 1024:1024 + ql], in_=psum_db[:, 0:ql])
                        assert sC.inc(ins) == c_db[j]

            @blk.vector
            def _(vector):
                for i in range(NWIN + 2):
                    if 1 <= i <= NWIN:
                        j = i - 1
                        h, ks, q0, ql = WIN(j)
                        row = (j % 2) * 32
                        vector.wait_ge(sT.h, t_d[j])
                        if j >= 2:
                            vector.wait_ge(sT.h, t_db[j - 2])
                        rh = rd_hl[j % 2]
                        with nc.allow_low_precision(reason="bf16 attention prob normalizer (probs are bf16)"):
                            ins = vector.reciprocal(rh[0:1, 0:ql], psum_d[row:row + 1, 0:ql])
                        assert sV.inc(ins) == v_rd[j]
                    if 2 <= i <= NWIN + 1:
                        j = i - 2
                        h, ks, q0, ql = WIN(j)
                        vector.wait_ge(sC.h, c_db[j])
                        vector.wait_ge(sT.h, t_ctx[j])
                        if j >= 2:
                            vector.wait_ge(sP.h, bP + nblend[j - 1])
                        ins = vector.tensor_mul(ctxn[j % 2][:, 0:ql], psum_ctx[j % 2][:, 0:ql],
                                                tmp[:, 1024:1024 + ql])
                        assert sV.inc(ins) == v_dc[j]

            @blk.gpsimd
            def _(gpsimd):
                first = True
                for j in range(NWIN):
                    wi = j % 5
                    if wi < 1:
                        continue
                    h = j // 5
                    if first:
                        gpsimd.wait_ge(sW.h, wAL)
                        first = False
                    gpsimd.wait_ge(sV.h, v_dc[j])
                    b = wi - 1
                    poff = 0 if b == 0 else 256
                    cur = ctxn[j % 2]
                    prev = ctxn[(j - 1) % 2]
                    gpsimd.tensor_sub(tmp2g[:, 0:256], cur[:, 0:256], prev[:, poff:poff + 256])
                    gpsimd.tensor_mul(tmp2g[:, 0:256], tmp2g[:, 0:256], fbC[:, b * 256:(b + 1) * 256])
                    ins = gpsimd.tensor_add(bigA[:, h * 1024 + b * 256:h * 1024 + (b + 1) * 256],
                                            tmp2g[:, 0:256], prev[:, poff:poff + 256])
                    assert sP.inc(ins) == bP + nblend[j]
        es5.close()

        es6s = ExitStack()
        psum_s2 = es6s.enter_context(nc.psum_tensor("psum_s2", [128, OWN], F32))
        psum_q2 = es6s.enter_context(nc.psum_tensor("psum_q2", [128, OWN], F32))
        es6 = ExitStack()
        psum_wo = [es6.enter_context(nc.psum_tensor(f"psum_wo{i}", [128, OWN], F32)) for i in range(2)]

        bT, bC, bV = sT.v, sC.v, sV.v
        bS1 = sS1.v
        t_mm6 = {}
        t_st6 = {}
        _tt = bT
        for oc in range(32):
            _tt += 1; t_mm6[oc] = _tt
            if oc >= 1:
                _tt += 1; t_st6[oc - 1] = _tt
        _tt += 1; t_st6[31] = _tt
        v_x1 = lambda oc: bV + oc + 1
        c_sb6 = lambda oc: bC + oc + 1
        bGs = [s.v for s in sG]
        g_after_p6 = []
        gtmp = list(bGs)
        for j in range(64):
            gtmp[j % 3] += 16
            g_after_p6.append(gtmp[j % 3])
        bXa, bXb = sXa.v, sXb.v
        x_after = lambda oc: (bXa if oc % 2 == 0 else bXb) + 16 * (oc // 2 + 1)

        with nc.named_scope("p6"), nc.Block("p6") as blk:
            @blk.gpsimd
            def _(gpsimd):
                for j in range(64):
                    if j >= 3:
                        gpsimd.wait_ge(sT.h, t_mm6[(j - 3) // 2])
                    slot = j % 3
                    oc, hf = j // 2, j % 2
                    ins = gpsimd.dma_start(out=wbf[:, slot * 2048:(slot + 1) * 2048],
                                           in_=wo[oc, :, hf * 2048:(hf + 1) * 2048])
                    assert sG[slot].inc(ins, 16) == g_after_p6[j]

            @blk.sync
            def _(sync):
                for oc in range(2):
                    sem = sXa if oc % 2 == 0 else sXb
                    fb = fbA if oc % 2 == 0 else fbB
                    assert sem.inc(sync.dma_start(out=fb[:, 0:1024],
                                                  in_=xT[oc * 128:(oc + 1) * 128, 256:1280]), 16) == x_after(oc)
                for oc in range(32):
                    sync.wait_ge(sV.h, v_x1(oc))
                    sS1.inc(sync.dma_start(out=x1scr[oc * 128:(oc + 1) * 128, :],
                                           in_=(fbC if oc % 2 == 0 else fbD)[:, :]), 16)
                    if oc + 2 < 32:
                        oc2 = oc + 2
                        sem = sXa if oc2 % 2 == 0 else sXb
                        fb = fbA if oc2 % 2 == 0 else fbB
                        assert sem.inc(sync.dma_start(out=fb[:, 0:1024],
                                                      in_=xT[oc2 * 128:(oc2 + 1) * 128, 256:1280]), 16) == x_after(oc2)

            @blk.tensor
            def _(tensor):
                def st_mms(j):
                    tensor.wait_ge(sC.h, c_sb6(j))
                    st_, sp_ = j == 0, j == 31
                    so = (j % 2) * 1024
                    x1b = bigB[:, j * 1024:(j + 1) * 1024]
                    tensor.matmul(psum_s2[0:1, 0:512], ones_c[:, :], x1b[:, 0:512], start=st_, stop=sp_)
                    tensor.matmul(psum_s2[0:1, 512:1024], ones_c[:, :], x1b[:, 512:1024], start=st_, stop=sp_)
                    tensor.matmul(psum_q2[0:1, 0:512], ones_c[:, :], sgbuf[:, so:so + 512], start=st_, stop=sp_)
                    ins = tensor.matmul(psum_q2[0:1, 512:1024], ones_c[:, :], sgbuf[:, so + 512:so + 1024], start=st_, stop=sp_)
                    assert sT.inc(ins) == t_st6[j]
                for oc in range(32):
                    if oc >= 2:
                        tensor.wait_ge(sV.h, v_x1(oc - 2))
                    pw = psum_wo[oc % 2]
                    for fs in range(NF):
                        if fs == 0:
                            tensor.wait_ge(sG[(2 * oc) % 3].h, g_after_p6[2 * oc])
                        if fs == 16:
                            tensor.wait_ge(sG[(2 * oc + 1) % 3].h, g_after_p6[2 * oc + 1])
                        slot = (2 * oc + fs // 16) % 3
                        lhs = wbf[:, slot * 2048 + (fs % 16) * 128:slot * 2048 + (fs % 16 + 1) * 128]
                        st_, sp_ = fs == 0, fs == NF - 1
                        tensor.matmul(pw[:, 0:512], lhs, bigA[:, fs * 1024:fs * 1024 + 512], start=st_, stop=sp_)
                        ins = tensor.matmul(pw[:, 512:1024], lhs, bigA[:, fs * 1024 + 512:fs * 1024 + 1024], start=st_, stop=sp_)
                    assert sT.inc(ins) == t_mm6[oc]
                    if oc >= 1:
                        st_mms(oc - 1)
                st_mms(31)

            @blk.vector
            def _(vector):
                for oc in range(32):
                    vector.wait_ge(sT.h, t_mm6[oc])
                    vector.wait_ge((sXa if oc % 2 == 0 else sXb).h, x_after(oc))
                    if oc >= 2:
                        vector.wait_ge(sS1.h, bS1 + 16 * (oc - 1))
                        vector.wait_ge(sC.h, c_sb6(oc - 2))
                    fb = fbA if oc % 2 == 0 else fbB
                    x1f = fbC if oc % 2 == 0 else fbD
                    ins = vector.tensor_add(x1f[:, :], psum_wo[oc % 2][:, :], fb[:, 0:1024])
                    assert sV.inc(ins) == v_x1(oc)

            @blk.scalar
            def _(scalar):
                for oc in range(32):
                    scalar.wait_ge(sV.h, v_x1(oc))
                    if oc >= 2:
                        scalar.wait_ge(sT.h, t_st6[oc - 2])
                    x1f = fbC if oc % 2 == 0 else fbD
                    so = (oc % 2) * 1024
                    scalar.copy(out=bigB[:, oc * 1024:(oc + 1) * 1024], in_=x1f[:, :])
                    ins = scalar.activation(out=sgbuf[:, so:so + 1024], in_=x1f[:, :], func=AF.Square)
                    assert sC.inc(ins) == c_sb6(oc)
        es6.close()

        m2 = fbA[0:1, 0:OWN]
        w2 = fbB[0:1, 0:OWN]
        hi2 = sgbuf[0:1, 0:OWN]
        lo2 = sgbuf[0:1, 1024:2048]
        ln_coef_block("p7a", psum_s2, psum_q2, OWN, m2, w2, hi2, lo2)
        es6s.close()

        bcast_block("p7b", OWN, m2, hi2, lo2)

        es7c = ExitStack()
        psum_r = es7c.enter_context(nc.psum_tensor("psum_r", [8, 1024], F32))
        psum_tr = es7c.enter_context(nc.psum_tensor("psum_tr", [128, 8], F32))
        psum_tw = es7c.enter_context(nc.psum_tensor("psum_tw", [8, 128], BF16))
        psum_we = es7c.enter_context(nc.psum_tensor("psum_we", [128, 1024], F32))
        psum_h = es7c.enter_context(nc.psum_tensor("psum_h", [128, 1024], F32))

        bT, bC, bV, bP7 = sT.v, sC.v, sV.v, sP.v
        vnorm7 = {}
        gnorm7 = {}
        _vv = bV
        for fs in range(NF):
            if fs not in GPSET:
                _vv += 1; vnorm7[fs] = _vv
        nvn7 = _vv
        _gg = bP7
        for fs in sorted(GPSET):
            _gg += 1; gnorm7[fs] = _gg
        t_r = bT + 1
        t_tr = lambda qt: bT + 2 + qt
        t_tw = lambda qt: bT + 10 + qt
        t_we = bT + 18
        t_h = bT + 19
        c_lg = bC + 1
        c_z = lambda qt: bC + 2 + qt
        c_wsb = lambda qt: bC + 10 + qt
        c_we = bC + 18
        v_w = lambda qt: nvn7 + qt + 1

        wALL7 = sW.v

        with nc.named_scope("p7c"), nc.Block("p7c") as blk:
            @blk.gpsimd
            def _(gpsimd):
                sG[0].inc(gpsimd.dma_start(out=wbf[:, 0:4096], in_=apk[:, :]), 16)
                for fs in sorted(GPSET):
                    a0 = fs * 1024
                    for hh in range(2):
                        o = a0 + hh * 512
                        gpsimd.tensor_mul(tmp2g[:, :], bigB[:, o:o + 512], a_sb[:, hh * 512:(hh + 1) * 512])
                        ins = gpsimd.tensor_add(bigB[:, o:o + 512], tmp2g[:, :], b_sb[:, hh * 512:(hh + 1) * 512])
                    assert sP.inc(ins) == gnorm7[fs]
            gAP = sG[0].v

            @blk.vector
            def _(vector):
                for fs in range(NF):
                    if fs in GPSET:
                        continue
                    a0 = fs * 1024
                    vector.tensor_mul(tmp[:, 0:1024], bigB[:, a0:a0 + 1024], a_sb[:, :])
                    ins = vector.tensor_add(bigB[:, a0:a0 + 1024], tmp[:, 0:1024], b_sb[:, :])
                    assert sV.inc(ins) == vnorm7[fs]

            @blk.tensor
            def _(tensor):
                tensor.wait_ge(sW.h, wALL7)
                for fs in range(NF):
                    st_, sp_ = fs == 0, fs == NF - 1
                    if fs in GPSET:
                        tensor.wait_ge(sP.h, gnorm7[fs])
                    else:
                        tensor.wait_ge(sV.h, vnorm7[fs])
                    lhs = wr_b[:, fs * 8:(fs + 1) * 8]
                    tensor.matmul(psum_r[0:8, 0:512], lhs, bigB[:, fs * 1024:fs * 1024 + 512], start=st_, stop=sp_)
                    ins = tensor.matmul(psum_r[0:8, 512:1024], lhs, bigB[:, fs * 1024 + 512:fs * 1024 + 1024], start=st_, stop=sp_)
                assert sT.inc(ins) == t_r
                tensor.wait_ge(sC.h, c_lg)
                for qt in range(8):
                    if qt >= 1:
                        tensor.wait_ge(sC.h, c_z(qt - 1))
                    ins = tensor.transpose(psum_tr[:, :], lg_sb[0:8, qt * 128:(qt + 1) * 128], idf8_s[:, :])
                    assert sT.inc(ins) == t_tr(qt)
                for qt in range(8):
                    tensor.wait_ge(sV.h, v_w(qt))
                    if qt >= 1:
                        tensor.wait_ge(sC.h, c_wsb(qt - 1))
                    ins = tensor.transpose(psum_tw[:, :], w_tm[:, qt * 8:(qt + 1) * 8], identb_s[:, :])
                    assert sT.inc(ins) == t_tw(qt)
                tensor.wait_ge(sC.h, c_wsb(7))
                tensor.matmul(psum_we[:, 0:512], eexp_s[:, :], w_sb[0:8, 0:512], start=True, stop=True)
                ins = tensor.matmul(psum_we[:, 512:1024], eexp_s[:, :], w_sb[0:8, 512:1024], start=True, stop=True)
                assert sT.inc(ins) == t_we
                tensor.wait_ge(sG[0].h, gAP)
                for fs in range(NF):
                    st_, sp_ = fs == 0, fs == NF - 1
                    lhs = wbf[:, fs * 128:(fs + 1) * 128]
                    tensor.matmul(psum_h[:, 0:512], lhs, bigB[:, fs * 1024:fs * 1024 + 512], start=st_, stop=sp_)
                    ins = tensor.matmul(psum_h[:, 512:1024], lhs, bigB[:, fs * 1024 + 512:fs * 1024 + 1024], start=st_, stop=sp_)
                assert sT.inc(ins) == t_h

            @blk.scalar
            def _(scalar):
                scalar.wait_ge(sT.h, t_r)
                ins = scalar.activation(out=lg_sb[0:8, 0:1024], in_=psum_r[0:8, :], func=AF.Identity, bias=br_s[0:8, 0:1])
                assert sC.inc(ins) == c_lg
                for qt in range(8):
                    scalar.wait_ge(sT.h, t_tr(qt))
                    ins = scalar.copy(out=z_sb[:, qt * 8:(qt + 1) * 8], in_=psum_tr[:, :])
                    assert sC.inc(ins) == c_z(qt)
                for qt in range(8):
                    scalar.wait_ge(sT.h, t_tw(qt))
                    ins = scalar.copy(out=w_sb[0:8, qt * 128:(qt + 1) * 128], in_=psum_tw[:, :])
                    assert sC.inc(ins) == c_wsb(qt)
                scalar.wait_ge(sT.h, t_we)
                ins = scalar.copy(out=tmp[:, 0:1024], in_=psum_we[:, :])
                assert sC.inc(ins) == c_we

            @blk.vector
            def _(vector):
                for qt in range(8):
                    vector.wait_ge(sC.h, c_z(qt))
                    z = z_sb[:, qt * 8:(qt + 1) * 8]
                    vector.memset(mk_sb[:, :], 1.0)
                    for it in range(8):
                        vector.tensor_mul(z8[:, :], z, mk_sb[:, :])
                        vector.reduce_sum(col_s[:, :], z8[:, :], axis=AX)
                        vector.reduce_sum(col_c[:, :], mk_sb[:, :], axis=AX)
                        vector.tensor_scalar_add(col_s[:, :], col_s[:, :], -1.0)
                        vector.reciprocal(col_c[:, :], col_c[:, :])
                        vector.tensor_mul(col_t[:, :], col_s[:, :], col_c[:, :])
                        if it < 7:
                            vector.tensor_single_scalar(mk_sb[:, :], z, col_t[:, :], OP.is_gt)
                    ins = vector.tensor_scalar(out=w_tm[:, qt * 8:(qt + 1) * 8], in0=z,
                                               scalar1=col_t[:, :], scalar2=0.0,
                                               op0=OP.subtract, op1=OP.max)
                    assert sV.inc(ins) == v_w(qt)
                vector.wait_ge(sT.h, t_h)
                vector.wait_ge(sC.h, c_we)
                sV.inc(vector.tensor_mul(u_bf[:, :], psum_h[:, :], tmp[:, 0:1024]))
        es7c.close()

        es8 = ExitStack()
        psum_o = [es8.enter_context(nc.psum_tensor(f"psum_o{i}", [128, OWN], F32)) for i in range(2)]

        bT, bV = sT.v, sV.v
        bS1_tot, bS2 = sS1.v, sS2.v
        t_mm8 = lambda oc: bT + oc + 1
        v_out = lambda oc: bV + oc + 1
        bGs = [s.v for s in sG]
        bG2 = sG[2].v
        g_after_p8 = []
        gtmp = [bGs[0], bGs[1], bGs[2] + 16]
        for j in range(64):
            gtmp[j % 3] += 16
            g_after_p8.append(gtmp[j % 3])
        bXa, bXb = sXa.v, sXb.v
        x_after8 = lambda oc: (bXa if oc % 2 == 0 else bXb) + 16 * (oc // 2 + 1)

        wALL8 = sW.v

        with nc.named_scope("p8"), nc.Block("p8") as blk:
            @blk.gpsimd
            def _(gpsimd):
                sG[2].inc(gpsimd.dma_start(out=sgbuf[:, :], in_=bpk[:, :]), 16)
                for j in range(64):
                    if j >= 3:
                        gpsimd.wait_ge(sT.h, t_mm8((j - 3) // 2))
                    slot = j % 3
                    oc, hf = j // 2, j % 2
                    ins = gpsimd.dma_start(out=wbf[:, slot * 2048:(slot + 1) * 2048],
                                           in_=wb[oc, :, hf * 2048:(hf + 1) * 2048])
                    assert sG[slot].inc(ins, 16) == g_after_p8[j]

            @blk.sync
            def _(sync):
                sync.wait_ge(sS1.h, bS1_tot)
                for oc in range(2):
                    sem = sXa if oc % 2 == 0 else sXb
                    fb = fbA if oc % 2 == 0 else fbB
                    assert sem.inc(sync.dma_start(out=fb[:, 0:1024],
                                                  in_=x1scr[oc * 128:(oc + 1) * 128, :]), 16) == x_after8(oc)
                for oc in range(32):
                    sync.wait_ge(sV.h, v_out(oc))
                    sS2.inc(sync.dma_start(out=outT[oc * 128:(oc + 1) * 128, :],
                                           in_=(fbC if oc % 2 == 0 else fbD)[:, :]), 16)
                    if oc + 2 < 32:
                        oc2 = oc + 2
                        sem = sXa if oc2 % 2 == 0 else sXb
                        fb = fbA if oc2 % 2 == 0 else fbB
                        assert sem.inc(sync.dma_start(out=fb[:, 0:1024],
                                                      in_=x1scr[oc2 * 128:(oc2 + 1) * 128, :]), 16) == x_after8(oc2)
                sync.wait_ge(sS2.h, bS2 + 16 * 32)

            @blk.tensor
            def _(tensor):
                tensor.wait_ge(sG[2].h, bG2 + 16)
                for oc in range(32):
                    if oc >= 2:
                        tensor.wait_ge(sV.h, v_out(oc - 2))
                    po = psum_o[oc % 2]
                    for fs in range(NF):
                        if fs == 0:
                            tensor.wait_ge(sG[(2 * oc) % 3].h, g_after_p8[2 * oc])
                        if fs == 16:
                            tensor.wait_ge(sG[(2 * oc + 1) % 3].h, g_after_p8[2 * oc + 1])
                        slot = (2 * oc + fs // 16) % 3
                        lhs = wbf[:, slot * 2048 + (fs % 16) * 128:slot * 2048 + (fs % 16 + 1) * 128]
                        st_ = fs == 0
                        tensor.matmul(po[:, 0:512], lhs, bigB[:, fs * 1024:fs * 1024 + 512], start=st_, stop=False)
                        tensor.matmul(po[:, 512:1024], lhs, bigB[:, fs * 1024 + 512:fs * 1024 + 1024], start=st_, stop=False)
                    blh = sgbuf[:, oc * 128:(oc + 1) * 128]
                    tensor.matmul(po[:, 0:512], blh, u_bf[:, 0:512], start=False, stop=True)
                    ins = tensor.matmul(po[:, 512:1024], blh, u_bf[:, 512:1024], start=False, stop=True)
                    assert sT.inc(ins) == t_mm8(oc)

            @blk.vector
            def _(vector):
                for oc in range(32):
                    if oc == 0:
                        vector.wait_ge(sW.h, wALL8)
                    vector.wait_ge(sT.h, t_mm8(oc))
                    vector.wait_ge((sXa if oc % 2 == 0 else sXb).h, x_after8(oc))
                    if oc >= 2:
                        vector.wait_ge(sS2.h, bS2 + 16 * (oc - 1))
                    fb = fbA if oc % 2 == 0 else fbB
                    outf = fbC if oc % 2 == 0 else fbD
                    ins = vector.scalar_tensor_tensor(outf[:, :], psum_o[oc % 2][:, :],
                                                      bb_s[:, oc:oc + 1], fb[:, 0:1024],
                                                      OP.add, OP.add)
                    assert sV.inc(ins) == v_out(oc)
        es8.close()

    return nc


def _rope_tables(dt):
    inv = 1.0 / (10000.0 ** (np.arange(0, DH, 2, dtype=np.float64) / DH))
    t = np.arange(S, dtype=np.float64)
    f = np.outer(t, inv)
    cos = np.zeros((S, DH), np.float64)
    sin = np.zeros((S, DH), np.float64)
    cos[:, 0::2] = np.cos(f); cos[:, 1::2] = np.cos(f)
    sin[:, 0::2] = np.sin(f); sin[:, 1::2] = np.sin(f)
    return cos.astype(dt), sin.astype(dt)


def kernel(L0_attn_x, L0_attn_Wq, L0_attn_Wk, L0_attn_Wv, L0_attn_Wo,
           L0_ffn_W_base, L0_ffn_b_base, L0_ffn_W_router, L0_ffn_b_router,
           L0_ffn_lora_A, L0_ffn_lora_B):
    x = np.asarray(L0_attn_x)
    dt = x.dtype
    bf = ml_dtypes.bfloat16

    def lay_big(W):
        return np.ascontiguousarray(
            np.asarray(W).reshape(32, 128, 32, 128).transpose(0, 3, 2, 1).reshape(32, 128, 4096))

    wq_l = lay_big(L0_attn_Wq)
    wo_l = lay_big(L0_attn_Wo)
    wb_l = lay_big(L0_ffn_W_base)
    wk_l = np.ascontiguousarray(np.asarray(L0_attn_Wk).reshape(128, 32, 128).transpose(2, 1, 0).reshape(128, 4096))
    wv_l = np.ascontiguousarray(np.asarray(L0_attn_Wv).reshape(128, 32, 128).transpose(2, 1, 0).reshape(128, 4096))
    wr_l = np.ascontiguousarray(np.asarray(L0_ffn_W_router).reshape(8, 32, 128).transpose(2, 1, 0).reshape(128, 256))
    apk_l = np.ascontiguousarray(np.asarray(L0_ffn_lora_A).transpose(1, 0, 2).reshape(4096, 128)
                                 .reshape(32, 128, 128).transpose(1, 0, 2).reshape(128, 4096))
    bpk_l = np.ascontiguousarray(np.asarray(L0_ffn_lora_B).reshape(128, 4096))
    bb_l = np.ascontiguousarray(np.asarray(L0_ffn_b_base).reshape(32, 128).T)
    br_l = np.ascontiguousarray(np.asarray(L0_ffn_b_router).reshape(8, 1))

    cosS, sinS = _rope_tables(dt)
    SCALE = dt.type(1.0) / np.sqrt(dt.type(DH))

    identb = np.eye(128).astype(bf)
    idf8 = np.eye(8).astype(dt)
    P = np.zeros((128, 128), np.float64)
    for i in range(64):
        P[i, i + 64] = -1.0
        P[i + 64, i] = 1.0
    ptr_l = np.ascontiguousarray(P.T).astype(bf)
    eexp_l = np.zeros((8, 128), np.float64)
    for e in range(8):
        eexp_l[e, e * 16:(e + 1) * 16] = 1.0
    eexp_l = eexp_l.astype(bf)

    xT_full = np.ascontiguousarray(x.reshape(S, D).T)
    base_alpha = np.tile(np.linspace(0.0, 1.0, 256, dtype=dt), 4)

    in_maps = []
    for c in range(NC):
        lo, hi = c * OWN, (c + 1) * OWN
        xTc = np.zeros((D, HALO), dt)
        s0, s1 = max(lo - 256, 0), min(hi + 256, S)
        xTc[:, s0 - (lo - 256):s1 - (lo - 256)] = xT_full[:, s0:s1]
        mask = np.full(HALO, -40.0, np.float64)
        mask[s0 - (lo - 256):s1 - (lo - 256)] = 0.0
        maskk_c = np.ascontiguousarray(mask.reshape(12, 128).T).astype(dt)
        al = base_alpha.copy()
        if c == 0:
            al[:256] = 1.0
        alpha_c = np.ascontiguousarray(np.broadcast_to(al, (128, OWN)))
        pos_k = np.clip(np.arange(lo - 256, hi + 256), 0, S - 1)
        in_maps.append({
            "xT": xTc,
            "wq": wq_l, "wo": wo_l, "wb": wb_l, "wk": wk_l, "wv": wv_l,
            "wr": wr_l, "apk": apk_l, "bpk": bpk_l, "bb": bb_l, "br": br_l,
            "cosq": np.ascontiguousarray((cosS[lo:hi] * SCALE).T),
            "sinq": np.ascontiguousarray((sinS[lo:hi] * SCALE).T),
            "cosk": np.ascontiguousarray(cosS[pos_k].T),
            "sink": np.ascontiguousarray(sinS[pos_k].T),
            "maskk": maskk_c, "alpha": alpha_c,
            "identb": identb, "idf8": idf8, "ptr": ptr_l, "eexp": eexp_l,
        })

    nc = build_nc()
    res = run_bass_kernel_spmd(nc, in_maps, core_ids=list(range(NC)))
    outs = [res.results[c]["outT"] for c in range(NC)]
    full = np.concatenate(outs, axis=1)
    return np.ascontiguousarray(full.T).reshape(1, S, D)


# version 07
# speedup vs baseline: 1.0080x; 1.0080x over previous
import numpy as np
import ml_dtypes
from contextlib import ExitStack

import concourse.bass as bass
import concourse.mybir as mybir
from concourse.bass_utils import run_bass_kernel_spmd

F32 = mybir.dt.float32
BF16 = mybir.dt.bfloat16
OP = mybir.AluOpType
AF = mybir.ActivationFunctionType
AX = mybir.AxisListType.X

S, D, NC, OWN, HALO = 8192, 4096, 8, 1024, 1536
H, DH, NF = 32, 128, 32
EPS = 1e-5
WINS = [(0, 0, 256), (256, 0, 512), (512, 256, 512), (768, 512, 512), (1024, 768, 256)]


class Sem:
    def __init__(self, ctx, nc, name):
        self.h = ctx.enter_context(nc.semaphore(name))
        self.v = 0

    def inc(self, instr, n=1):
        instr.then_inc(self.h, n)
        self.v += n
        return self.v


def build_nc():
    nc = bass.Bass("TRN2", target_bir_lowering=False)

    xT = nc.dram_tensor("xT", [D, HALO], F32, kind="ExternalInput")
    wq = nc.dram_tensor("wq", [32, 128, 4096], F32, kind="ExternalInput")
    wo = nc.dram_tensor("wo", [32, 128, 4096], F32, kind="ExternalInput")
    wb = nc.dram_tensor("wb", [32, 128, 4096], F32, kind="ExternalInput")
    wk = nc.dram_tensor("wk", [128, 4096], F32, kind="ExternalInput")
    wv = nc.dram_tensor("wv", [128, 4096], F32, kind="ExternalInput")
    wr = nc.dram_tensor("wr", [128, 256], F32, kind="ExternalInput")
    apk = nc.dram_tensor("apk", [128, 4096], F32, kind="ExternalInput")
    bpk = nc.dram_tensor("bpk", [128, 4096], F32, kind="ExternalInput")
    bb = nc.dram_tensor("bb", [128, 32], F32, kind="ExternalInput")
    br = nc.dram_tensor("br", [8, 1], F32, kind="ExternalInput")
    cosq = nc.dram_tensor("cosq", [128, OWN], F32, kind="ExternalInput")
    sinq = nc.dram_tensor("sinq", [128, OWN], F32, kind="ExternalInput")
    cosk = nc.dram_tensor("cosk", [128, HALO], F32, kind="ExternalInput")
    sink = nc.dram_tensor("sink", [128, HALO], F32, kind="ExternalInput")
    maskk = nc.dram_tensor("maskk", [128, 12], F32, kind="ExternalInput")
    alpha = nc.dram_tensor("alpha", [128, OWN], F32, kind="ExternalInput")
    identb = nc.dram_tensor("identb", [128, 128], BF16, kind="ExternalInput")
    idf8 = nc.dram_tensor("idf8", [8, 8], F32, kind="ExternalInput")
    ptr = nc.dram_tensor("ptr", [128, 128], BF16, kind="ExternalInput")
    eexp = nc.dram_tensor("eexp", [8, 128], BF16, kind="ExternalInput")
    x1scr = nc.dram_tensor("x1scr", [D, OWN], F32)
    outT = nc.dram_tensor("outT", [D, OWN], F32, kind="ExternalOutput")

    ctx = ExitStack()
    with ctx:
        sb = lambda n, s, d: ctx.enter_context(nc.sbuf_tensor(n, s, d))
        bigA = sb("bigA", [128, 32 * 1024], BF16)
        bigB = sb("bigB", [128, 32 * 1024], BF16)
        sgbuf = sb("sgbuf", [128, 4096], BF16)
        wbf = sb("wbf", [128, 3 * 2048], BF16)
        fbA = sb("fbA", [128, HALO], F32)
        fbB = sb("fbB", [128, HALO], F32)
        fbC = sb("fbC", [128, OWN], F32)
        fbD = sb("fbD", [128, OWN], F32)
        kr_s = sb("kr_s", [128, HALO], BF16)
        ku_s = sb("ku_s", [128, HALO], BF16)
        vtm_s = sb("vtm_s", [128, HALO], BF16)
        tmp = sb("tmp", [128, HALO], F32)
        ctxn = [sb("ctxn0", [128, 512], F32), sb("ctxn1", [128, 512], F32)]
        u_bf = sb("u_bf", [128, OWN], BF16)
        wr_b = sb("wr_b", [128, 256], BF16)
        tmp2g = sb("tmp2g", [128, 512], F32)
        a_sb = sb("a_sb", [128, 1024], F32)
        b_sb = sb("b_sb", [128, 1024], F32)
        rd_hl = [sb("rd_h0", [1, 512], BF16), sb("rd_h1", [1, 512], BF16)]
        z_sb = sb("z_sb", [128, 64], F32)
        z8 = sb("z8", [128, 8], F32)
        mk_sb = sb("mk_sb", [128, 8], F32)
        col_s = sb("col_s", [128, 1], F32)
        col_c = sb("col_c", [128, 1], F32)
        col_t = sb("col_t", [128, 1], F32)
        w_tm = sb("w_tm", [128, 64], BF16)
        w_sb = sb("w_sb", [8, 1024], BF16)
        identb_s = sb("identb_s", [128, 128], BF16)
        idf8_s = sb("idf8_s", [8, 8], F32)
        ptr_s = sb("ptr_s", [128, 128], BF16)
        eexp_s = sb("eexp_s", [8, 128], BF16)
        ones_c = sb("ones_c", [128, 1], BF16)
        ones_r = sb("ones_r", [1, 128], BF16)
        maskk_s = sb("maskk_s", [128, 12], F32)
        bb_s = sb("bb_s", [128, 32], F32)
        br_s = sb("br_s", [8, 1], F32)
        eps_s = sb("eps_s", [1, 1], F32)
        lg_sb = fbA
        assert nc.sbuf_bytes_remaining > 512, nc.sbuf_bytes_remaining

        EDG = 0
        WKO = 16 * 1024
        WVO = 20 * 1024

        sW = Sem(ctx, nc, "sW")
        sXa = Sem(ctx, nc, "sXa")
        sXb = Sem(ctx, nc, "sXb")
        sG = [Sem(ctx, nc, f"sG{i}") for i in range(3)]
        sS1 = Sem(ctx, nc, "sS1")
        sS2 = Sem(ctx, nc, "sS2")
        sP = Sem(ctx, nc, "sP")
        sT = Sem(ctx, nc, "sT")
        sV = Sem(ctx, nc, "sV")
        sC = Sem(ctx, nc, "sC")

        es1 = ExitStack()
        psum_sum = es1.enter_context(nc.psum_tensor("psum_sum", [128, HALO], F32))
        psum_sq = es1.enter_context(nc.psum_tensor("psum_sq", [128, HALO], F32))

        bT, bC = sT.v, sC.v
        grp_sems = [sG[0], sG[1], sG[2], sXa]
        grp_after = [sem.v + 48 for sem in grp_sems]
        bXB = sXb.v
        sc_after = lambda fs: bC + fs + 1
        tt_after = lambda fs: bT + fs + 1

        with nc.named_scope("p1"), nc.Block("p1") as blk:
            @blk.sync
            def _(sync):
                for dst, src in ((identb_s, identb), (idf8_s, idf8), (ptr_s, ptr),
                                 (eexp_s, eexp), (maskk_s, maskk), (bb_s, bb),
                                 (br_s, br)):
                    sW.inc(sync.dma_start(out=dst[:, :], in_=src[:, :]), 16)

            @blk.vector
            def _(vector):
                vector.memset(ones_c[:, :], 1.0)
                vector.memset(eps_s[:, :], EPS)
                sV.inc(vector.memset(ones_r[:, :], 1.0))
            vONES = sV.v

            @blk.gpsimd
            def _(gpsimd):
                for g in range(4):
                    own_src = bass.AP(xT, g * 1024 * HALO + 256,
                                      [[HALO, 128], [128 * HALO, 8], [1, 1024]])
                    own_dst = bass.AP(bigA, g * 8192,
                                      [[32 * 1024, 128], [1024, 8], [1, 1024]])
                    grp_sems[g].inc(gpsimd.dma_start(out=own_dst, in_=own_src), 16)
                    for side in range(2):
                        edg_src = bass.AP(xT, g * 1024 * HALO + side * 1280,
                                          [[HALO, 128], [128 * HALO, 8], [1, 256]])
                        edg_dst = bass.AP(bigB, EDG + g * 4096 + side * 256,
                                          [[32 * 1024, 128], [512, 8], [1, 256]])
                        grp_sems[g].inc(gpsimd.dma_start(out=edg_dst, in_=edg_src), 16)
                sXb.inc(gpsimd.dma_start(out=wr_b[:, :], in_=wr[:, :]), 16)
                sXb.inc(gpsimd.dma_start(out=bigB[:, WKO:WKO + 4096], in_=wk[:, :]), 16)
                sXb.inc(gpsimd.dma_start(out=bigB[:, WVO:WVO + 4096], in_=wv[:, :]), 16)

            @blk.scalar
            def _(scalar):
                for fs in range(NF):
                    if fs % 8 == 0:
                        scalar.wait_ge(grp_sems[fs // 8].h, grp_after[fs // 8])
                    if fs >= 2:
                        scalar.wait_ge(sT.h, tt_after(fs - 2))
                    so = (fs % 2) * 2048
                    scalar.activation(out=sgbuf[:, so:so + 1024],
                                      in_=bigA[:, fs * 1024:(fs + 1) * 1024], func=AF.Square)
                    ins = scalar.activation(out=sgbuf[:, so + 1024:so + 1536],
                                            in_=bigB[:, EDG + fs * 512:EDG + fs * 512 + 512],
                                            func=AF.Square)
                    assert sC.inc(ins) == sc_after(fs)

            @blk.tensor
            def _(tensor):
                for fs in range(NF):
                    st_, sp_ = fs == 0, fs == NF - 1
                    if fs == 0:
                        tensor.wait_ge(sV.h, vONES)
                    if fs % 8 == 0:
                        tensor.wait_ge(grp_sems[fs // 8].h, grp_after[fs // 8])
                    tensor.wait_ge(sC.h, sc_after(fs))
                    so = (fs % 2) * 2048
                    a0 = fs * 1024
                    e0 = EDG + fs * 512
                    tensor.matmul(psum_sum[0:1, 0:512], ones_c[:, :], bigA[:, a0:a0 + 512], start=st_, stop=sp_)
                    tensor.matmul(psum_sum[0:1, 512:1024], ones_c[:, :], bigA[:, a0 + 512:a0 + 1024], start=st_, stop=sp_)
                    tensor.matmul(psum_sum[0:1, 1024:1536], ones_c[:, :], bigB[:, e0:e0 + 512], start=st_, stop=sp_)
                    tensor.matmul(psum_sq[0:1, 0:512], ones_c[:, :], sgbuf[:, so:so + 512], start=st_, stop=sp_)
                    tensor.matmul(psum_sq[0:1, 512:1024], ones_c[:, :], sgbuf[:, so + 512:so + 1024], start=st_, stop=sp_)
                    ins = tensor.matmul(psum_sq[0:1, 1024:1536], ones_c[:, :], sgbuf[:, so + 1024:so + 1536], start=st_, stop=sp_)
                    assert sT.inc(ins) == tt_after(fs)

        def ln_coef_block(name, psA, psB, n, m_row, w_row, hi_row, lo_row):
            with nc.named_scope(name), nc.Block(name) as blk2:
                @blk2.vector
                def _(vector):
                    vector.tensor_scalar_mul(m_row, psA[0:1, 0:n], 1.0 / D)
                    vector.scalar_tensor_tensor(w_row, m_row, -1.0, m_row, OP.mult, OP.mult)
                    ins = vector.scalar_tensor_tensor(w_row, psB[0:1, 0:n], 1.0 / D, w_row, OP.mult, OP.add)
                    sV.inc(ins)
                vS = sV.v

                @blk2.scalar
                def _(scalar):
                    scalar.wait_ge(sV.h, vS)
                    sC.inc(scalar.activation(out=w_row, in_=w_row, func=AF.Sqrt, bias=eps_s[0:1, 0:1]))
                cS = sC.v

                @blk2.vector
                def _(vector):
                    vector.wait_ge(sC.h, cS)
                    vector.reciprocal(w_row, w_row)
                    vector.tensor_mul(m_row, m_row, w_row)
                    vector.tensor_scalar_mul(m_row, m_row, -1.0)
                    vector.tensor_copy(out=hi_row, in_=w_row)
                    vector.tensor_sub(lo_row, w_row, hi_row)
                    sV.inc(vector.engine_nop())

        m1 = fbA[0:1, 0:HALO]
        w1 = fbB[0:1, 0:HALO]
        hi1 = sgbuf[0:1, 0:HALO]
        lo1 = sgbuf[0:1, 2048:2048 + HALO]
        ln_coef_block("p2a", psum_sum, psum_sq, HALO, m1, w1, hi1, lo1)
        es1.close()


        def bcast_block(name, n, m_row, hi_row, lo_row):
            es2o = ExitStack()
            psA = es2o.enter_context(nc.psum_tensor(name + "_pa", [128, 1024], F32))
            psB = es2o.enter_context(nc.psum_tensor(name + "_pb", [128, 1024], F32))
            if n > 1024:
                edge_ps = (es2o.enter_context(nc.psum_tensor(name + "_pae", [128, 512], F32)),
                           es2o.enter_context(nc.psum_tensor(name + "_pbe", [128, 512], F32)))
            with nc.named_scope(name), nc.Block(name) as blk2:
                @blk2.tensor
                def _(tensor):
                    for c in range(0, n, 512):
                        w = min(512, n - c)
                        dst = psA[:, c:c + w] if c < 1024 else edge_ps[0][:, c - 1024:c - 1024 + w]
                        tensor.matmul(dst, ones_r[:, :], hi_row[0:1, c:c + w], start=True, stop=False)
                        ins = tensor.matmul(dst, ones_r[:, :], lo_row[0:1, c:c + w], start=False, stop=True)
                    sT.inc(ins)
                tS = sT.v

                @blk2.vector
                def _(vector):
                    vector.wait_ge(sT.h, tS)
                    vector.tensor_copy(out=hi_row, in_=m_row)
                    vector.tensor_sub(lo_row, m_row, hi_row)
                    sV.inc(vector.engine_nop())
                vS = sV.v

                @blk2.tensor
                def _(tensor):
                    tensor.wait_ge(sV.h, vS)
                    for c in range(0, n, 512):
                        w = min(512, n - c)
                        dst = psB[:, c:c + w] if c < 1024 else edge_ps[1][:, c - 1024:c - 1024 + w]
                        tensor.matmul(dst, ones_r[:, :], hi_row[0:1, c:c + w], start=True, stop=False)
                        ins = tensor.matmul(dst, ones_r[:, :], lo_row[0:1, c:c + w], start=False, stop=True)
                    sT.inc(ins)
                tS2 = sT.v

                @blk2.scalar
                def _(scalar):
                    scalar.wait_ge(sT.h, tS2)
                    scalar.copy(out=a_sb[:, :], in_=psA[:, :])
                    ins = scalar.copy(out=b_sb[:, :], in_=psB[:, :])
                    if n > 1024:
                        scalar.copy(out=ctxn[0][:, :], in_=edge_ps[0][:, :])
                        ins = scalar.copy(out=ctxn[1][:, :], in_=edge_ps[1][:, :])
                    sC.inc(ins)
            es2o.close()

        bcast_block("p2b", HALO, m1, hi1, lo1)

        GPSET = set(fs for fs in range(NF) if fs % 3 == 0)
        bV3, bP3 = sV.v, sP.v
        vnorm = {}
        gnorm = {}
        _vv = bV3
        for fs in range(NF):
            _vv += 1; vnorm[fs] = _vv
        _gg = bP3
        for fs in sorted(GPSET):
            _gg += 1; gnorm[fs] = _gg

        es3v = ExitStack()
        psum_v = es3v.enter_context(nc.psum_tensor("psum_v", [128, HALO], F32))
        es3 = ExitStack()
        psum_k = es3.enter_context(nc.psum_tensor("psum_k", [128, HALO], F32))

        with nc.named_scope("p3kv"), nc.Block("p3kv") as blk:
            @blk.sync
            def _(sync):
                sW.inc(sync.dma_start(out=fbA[:, :], in_=cosk[:, :]), 16)
                sW.inc(sync.dma_start(out=fbB[:, :], in_=sink[:, :]), 16)
            wCK = sW.v

            @blk.vector
            def _(vector):
                for fs in range(NF):
                    a0 = fs * 1024
                    e0 = EDG + fs * 512
                    if fs not in GPSET:
                        vector.tensor_mul(tmp[:, 0:1024], bigA[:, a0:a0 + 1024], a_sb[:, :])
                        vector.tensor_add(bigA[:, a0:a0 + 1024], tmp[:, 0:1024], b_sb[:, :])
                    vector.tensor_mul(tmp[:, 1024:1536], bigB[:, e0:e0 + 512], ctxn[0][:, :])
                    ins = vector.tensor_add(bigB[:, e0:e0 + 512], tmp[:, 1024:1536], ctxn[1][:, :])
                    assert sV.inc(ins) == vnorm[fs]

            @blk.gpsimd
            def _(gpsimd):
                for fs in sorted(GPSET):
                    a0 = fs * 1024
                    for hh in range(2):
                        o = a0 + hh * 512
                        gpsimd.tensor_mul(tmp2g[:, :], bigA[:, o:o + 512], a_sb[:, hh * 512:(hh + 1) * 512])
                        ins = gpsimd.tensor_add(bigA[:, o:o + 512], tmp2g[:, :], b_sb[:, hh * 512:(hh + 1) * 512])
                    assert sP.inc(ins) == gnorm[fs]

            @blk.tensor
            def _(tensor):
                tensor.wait_ge(sXb.h, bXB + 48)
                for fs in range(NF):
                    st_, sp_ = fs == 0, fs == NF - 1
                    a0 = fs * 1024
                    e0 = EDG + fs * 512
                    tensor.wait_ge(sV.h, vnorm[fs])
                    if fs in GPSET:
                        tensor.wait_ge(sP.h, gnorm[fs])
                    for ps, woff in ((psum_k, WKO), (psum_v, WVO)):
                        lhs = bigB[:, woff + fs * 128:woff + (fs + 1) * 128]
                        ps_ = ps
                        tensor.matmul(ps_[:, 0:512], lhs, bigA[:, a0:a0 + 512], start=st_, stop=sp_)
                        tensor.matmul(ps_[:, 512:1024], lhs, bigA[:, a0 + 512:a0 + 1024], start=st_, stop=sp_)
                        ins = tensor.matmul(ps_[:, 1024:1536], lhs, bigB[:, e0:e0 + 512], start=st_, stop=sp_)
                sT.inc(ins)
            tKV = sT.v

            @blk.scalar
            def _(scalar):
                scalar.wait_ge(sT.h, tKV)
                scalar.copy(out=ku_s[:, 256:1280], in_=psum_k[:, 0:1024])
                scalar.copy(out=ku_s[:, 0:256], in_=psum_k[:, 1024:1280])
                sC.inc(scalar.copy(out=ku_s[:, 1280:1536], in_=psum_k[:, 1280:1536]))
            cK = sC.v
        es3.close()

        es3r = ExitStack()
        psum_rot = es3r.enter_context(nc.psum_tensor("psum_rot", [128, HALO], F32))
        psum_t = [es3r.enter_context(nc.psum_tensor(f"psum_t{i}", [128, 128], BF16)) for i in range(2)]

        wALL3 = sW.v

        with nc.named_scope("p3r"), nc.Block("p3r") as blk:
            @blk.tensor
            def _(tensor):
                tensor.wait_ge(sW.h, wALL3)
                tensor.wait_ge(sC.h, cK)
                for c in range(0, HALO, 512):
                    ins = tensor.matmul(psum_rot[:, c:c + 512], ptr_s[:, :], ku_s[:, c:c + 512], start=True, stop=True)
                sT.inc(ins)
            tROT = sT.v

            @blk.vector
            def _(vector):
                vector.wait_ge(sW.h, wCK)
                vector.tensor_mul(tmp[:, :], ku_s[:, :], fbA[:, :])
                vector.wait_ge(sT.h, tROT)
                vector.tensor_tensor(psum_rot[:, :], psum_rot[:, :], fbB[:, :], op=OP.mult)
                vector.tensor_add(kr_s[:, :], tmp[:, :], psum_rot[:, :])
                sV.inc(vector.engine_nop())
            vKR = sV.v

            @blk.scalar
            def _(scalar):
                scalar.wait_ge(sV.h, vKR)
                scalar.copy(out=ku_s[:, 256:1280], in_=psum_v[:, 0:1024])
                scalar.copy(out=ku_s[:, 0:256], in_=psum_v[:, 1024:1280])
                sC.inc(scalar.copy(out=ku_s[:, 1280:1536], in_=psum_v[:, 1280:1536]))
            cV = sC.v

            cv_pre = [cV + 1 + j for j in range(12)]
            tv_pre = [tROT + 1 + j for j in range(12)]

            @blk.tensor
            def _(tensor):
                tensor.wait_ge(sC.h, cV)
                for j in range(12):
                    if j >= 2:
                        tensor.wait_ge(sC.h, cv_pre[j - 2])
                    ins = tensor.transpose(psum_t[j % 2][:, :], ku_s[:, j * 128:(j + 1) * 128], identb_s[:, :])
                    assert sT.inc(ins) == tv_pre[j]

            @blk.scalar
            def _(scalar):
                for j in range(12):
                    scalar.wait_ge(sT.h, tv_pre[j])
                    ins = scalar.copy(out=vtm_s[:, j * 128:(j + 1) * 128], in_=psum_t[j % 2][:, :])
                    assert sC.inc(ins) == cv_pre[j]
        es3r.close()
        es3v.close()

        es4 = ExitStack()
        psum_q = [es4.enter_context(nc.psum_tensor(f"psum_q{i}", [128, OWN], F32)) for i in range(2)]
        psum_rq = es4.enter_context(nc.psum_tensor("psum_rq", [128, OWN], F32))

        bT, bC, bV = sT.v, sC.v, sV.v
        t_q = {}
        t_rot = {}
        _tt = bT
        for h in range(H):
            _tt += 1; t_q[h] = _tt
            if h >= 1:
                _tt += 1; t_rot[h - 1] = _tt
        _tt += 1; t_rot[H - 1] = _tt
        c_u = lambda h: bC + h + 1
        v_qr = lambda h: bV + h + 1
        bGs = [s.v for s in sG]
        g_after_p4 = []
        gtmp = list(bGs)
        for j in range(64):
            gtmp[j % 3] += 16
            g_after_p4.append(gtmp[j % 3])

        with nc.named_scope("p4"), nc.Block("p4") as blk:
            @blk.sync
            def _(sync):
                sW.inc(sync.dma_start(out=fbC[:, :], in_=cosq[:, :]), 16)
                sW.inc(sync.dma_start(out=fbD[:, :], in_=sinq[:, :]), 16)
            wCQ = sW.v

            @blk.gpsimd
            def _(gpsimd):
                for j in range(64):
                    if j >= 3:
                        gpsimd.wait_ge(sT.h, t_q[(j - 3) // 2])
                    slot = j % 3
                    oc, hf = j // 2, j % 2
                    ins = gpsimd.dma_start(
                        out=wbf[:, slot * 2048:(slot + 1) * 2048],
                        in_=wq[oc, :, hf * 2048:(hf + 1) * 2048])
                    assert sG[slot].inc(ins, 16) == g_after_p4[j]

            @blk.tensor
            def _(tensor):
                def rot_mms(j):
                    tensor.wait_ge(sC.h, c_u(j))
                    if j >= 1:
                        tensor.wait_ge(sV.h, v_qr(j - 1))
                    tensor.matmul(psum_rq[:, 0:512], ptr_s[:, :], u_bf[:, 0:512], start=True, stop=True)
                    ins = tensor.matmul(psum_rq[:, 512:1024], ptr_s[:, :], u_bf[:, 512:1024], start=True, stop=True)
                    assert sT.inc(ins) == t_rot[j]
                for h in range(H):
                    if h >= 2:
                        tensor.wait_ge(sC.h, c_u(h - 2))
                    pq = psum_q[h % 2]
                    for fs in range(NF):
                        if fs == 0:
                            tensor.wait_ge(sG[(2 * h) % 3].h, g_after_p4[2 * h])
                        if fs == 16:
                            tensor.wait_ge(sG[(2 * h + 1) % 3].h, g_after_p4[2 * h + 1])
                        slot = (2 * h + fs // 16) % 3
                        lhs = wbf[:, slot * 2048 + (fs % 16) * 128:slot * 2048 + (fs % 16 + 1) * 128]
                        st_, sp_ = fs == 0, fs == NF - 1
                        tensor.matmul(pq[:, 0:512], lhs, bigA[:, fs * 1024:fs * 1024 + 512], start=st_, stop=sp_)
                        ins = tensor.matmul(pq[:, 512:1024], lhs, bigA[:, fs * 1024 + 512:fs * 1024 + 1024], start=st_, stop=sp_)
                    assert sT.inc(ins) == t_q[h]
                    if h >= 1:
                        rot_mms(h - 1)
                rot_mms(H - 1)

            @blk.scalar
            def _(scalar):
                for h in range(H):
                    scalar.wait_ge(sT.h, t_q[h])
                    if h >= 1:
                        scalar.wait_ge(sV.h, v_qr(h - 1))
                        scalar.wait_ge(sT.h, t_rot[h - 1])
                    ins = scalar.copy(out=u_bf[:, :], in_=psum_q[h % 2][:, :])
                    assert sC.inc(ins) == c_u(h)

            @blk.vector
            def _(vector):
                for h in range(H):
                    vector.wait_ge(sC.h, c_u(h))
                    if h == 0:
                        vector.wait_ge(sW.h, wCQ)
                    vector.tensor_mul(tmp[:, 0:1024], u_bf[:, :], fbC[:, :])
                    vector.wait_ge(sT.h, t_rot[h])
                    vector.tensor_tensor(psum_rq[:, :], psum_rq[:, :], fbD[:, :], op=OP.mult)
                    ins = vector.tensor_add(bigB[:, h * 1024:(h + 1) * 1024], tmp[:, 0:1024], psum_rq[:, :])
                    assert sV.inc(ins) == v_qr(h)
        es4.close()

        es5 = ExitStack()
        psum_Sb = es5.enter_context(nc.psum_tensor("psum_Sb", [128, 2048], F32))
        psum_S = [psum_Sb[:, kt * 512:(kt + 1) * 512] for kt in range(4)]
        psum_ctx = [es5.enter_context(nc.psum_tensor(f"psum_ctx{i}", [128, 512], F32)) for i in range(2)]
        psum_d = es5.enter_context(nc.psum_tensor("psum_d", [128, 512], F32))
        psum_db = es5.enter_context(nc.psum_tensor("psum_db", [128, 512], F32))

        NWIN = H * 5
        WIN = lambda j: (j // 5,) + WINS[j % 5]
        bT, bC, bV = sT.v, sC.v, sV.v
        t_s = {}
        t_d = {}
        t_ctx = {}
        t_db = {}
        _tt = bT
        for i in range(NWIN + 2):
            if i < NWIN:
                _tt += 1; t_s[i] = _tt
            if 1 <= i <= NWIN:
                _tt += 1; t_d[i - 1] = _tt
                _tt += 1; t_ctx[i - 1] = _tt
            if 2 <= i <= NWIN + 1:
                _tt += 1; t_db[i - 2] = _tt
        c_sig = {}
        c_db = {}
        _cc = bC
        for i in range(NWIN + 2):
            if i < NWIN:
                _cc += 1; c_sig[i] = _cc
            if 2 <= i <= NWIN + 1:
                _cc += 1; c_db[i - 2] = _cc
        v_rd = {}
        v_dc = {}
        _vv = bV
        for i in range(NWIN + 2):
            if 1 <= i <= NWIN:
                _vv += 1; v_rd[i - 1] = _vv
            if 2 <= i <= NWIN + 1:
                _vv += 1; v_dc[i - 2] = _vv
        nblend = {-1: 0}
        _nb = 0
        for j in range(NWIN):
            if j % 5 >= 1:
                _nb += 1
            nblend[j] = _nb
        bP = sP.v

        with nc.named_scope("p5"), nc.Block("p5") as blk:
            @blk.sync
            def _(sync):
                sW.inc(sync.dma_start(out=fbC[:, :], in_=alpha[:, :]), 16)
            wAL = sW.v

            @blk.tensor
            def _(tensor):
                for i in range(NWIN + 2):
                    if i < NWIN:
                        h, ks, q0, ql = WIN(i)
                        qr = bigB[:, h * 1024 + q0:h * 1024 + q0 + ql]
                        if i >= 1:
                            tensor.wait_ge(sC.h, c_sig[i - 1])
                        for kt in range(4):
                            ins = tensor.matmul(psum_S[kt][:, 0:ql], kr_s[:, ks + kt * 128:ks + (kt + 1) * 128],
                                                qr, start=True, stop=True)
                        assert sT.inc(ins) == t_s[i]
                    if 1 <= i <= NWIN:
                        j = i - 1
                        h, ks, q0, ql = WIN(j)
                        sl = (j % 2) * 2048
                        if i == 1:
                            tensor.wait_ge(sC.h, c_sig[0])
                        if j >= 2:
                            tensor.wait_ge(sV.h, v_rd[j - 2])
                        row = (j % 2) * 32
                        for kt in range(4):
                            ins = tensor.matmul(psum_d[row:row + 1, 0:ql], ones_c[:, :],
                                                sgbuf[:, sl + kt * 512:sl + kt * 512 + ql],
                                                start=kt == 0, stop=kt == 3)
                        assert sT.inc(ins) == t_d[j]
                        if j >= 2:
                            tensor.wait_ge(sV.h, v_dc[j - 2])
                        pc = psum_ctx[j % 2]
                        for kt in range(4):
                            ins = tensor.matmul(pc[:, 0:ql], vtm_s[:, ks + kt * 128:ks + (kt + 1) * 128],
                                                sgbuf[:, sl + kt * 512:sl + kt * 512 + ql],
                                                start=kt == 0, stop=kt == 3)
                        assert sT.inc(ins) == t_ctx[j]
                    if 2 <= i <= NWIN + 1:
                        j = i - 2
                        h, ks, q0, ql = WIN(j)
                        tensor.wait_ge(sV.h, v_rd[j])
                        if j >= 1:
                            tensor.wait_ge(sC.h, c_db[j - 1])
                        rh = rd_hl[j % 2]
                        ins = tensor.matmul(psum_db[:, 0:ql], ones_r[:, :], rh[0:1, 0:ql], start=True, stop=True)
                        assert sT.inc(ins) == t_db[j]

            @blk.scalar
            def _(scalar):
                for i in range(NWIN + 2):
                    if i < NWIN:
                        h, ks, q0, ql = WIN(i)
                        sl = (i % 2) * 2048
                        if i == 0:
                            scalar.wait_ge(sW.h, wAL)
                        scalar.wait_ge(sT.h, t_s[i])
                        if i >= 2:
                            scalar.wait_ge(sT.h, t_ctx[i - 2])
                        if i % 5 in (1, 2, 3):
                            ins = scalar.activation(out=sgbuf[:, sl:sl + 2048],
                                                    in_=psum_Sb[:, :], func=AF.Sigmoid)
                        else:
                            for kt in range(4):
                                kc = ks // 128 + kt
                                ins = scalar.activation(out=sgbuf[:, sl + kt * 512:sl + kt * 512 + ql],
                                                        in_=psum_S[kt][:, 0:ql],
                                                        func=AF.Sigmoid, bias=maskk_s[:, kc:kc + 1], scale=1.0)
                        assert sC.inc(ins) == c_sig[i]
                    if 2 <= i <= NWIN + 1:
                        j = i - 2
                        h, ks, q0, ql = WIN(j)
                        scalar.wait_ge(sT.h, t_db[j])
                        if j >= 1:
                            scalar.wait_ge(sV.h, v_dc[j - 1])
                        ins = scalar.copy(out=tmp[:, 1024:1024 + ql], in_=psum_db[:, 0:ql])
                        assert sC.inc(ins) == c_db[j]

            @blk.vector
            def _(vector):
                for i in range(NWIN + 2):
                    if 1 <= i <= NWIN:
                        j = i - 1
                        h, ks, q0, ql = WIN(j)
                        row = (j % 2) * 32
                        vector.wait_ge(sT.h, t_d[j])
                        if j >= 2:
                            vector.wait_ge(sT.h, t_db[j - 2])
                        rh = rd_hl[j % 2]
                        with nc.allow_low_precision(reason="bf16 attention prob normalizer (probs are bf16)"):
                            ins = vector.reciprocal(rh[0:1, 0:ql], psum_d[row:row + 1, 0:ql])
                        assert sV.inc(ins) == v_rd[j]
                    if 2 <= i <= NWIN + 1:
                        j = i - 2
                        h, ks, q0, ql = WIN(j)
                        vector.wait_ge(sC.h, c_db[j])
                        vector.wait_ge(sT.h, t_ctx[j])
                        if j >= 2:
                            vector.wait_ge(sP.h, bP + nblend[j - 1])
                        ins = vector.tensor_mul(ctxn[j % 2][:, 0:ql], psum_ctx[j % 2][:, 0:ql],
                                                tmp[:, 1024:1024 + ql])
                        assert sV.inc(ins) == v_dc[j]

            @blk.gpsimd
            def _(gpsimd):
                first = True
                for j in range(NWIN):
                    wi = j % 5
                    if wi < 1:
                        continue
                    h = j // 5
                    if first:
                        gpsimd.wait_ge(sW.h, wAL)
                        first = False
                    gpsimd.wait_ge(sV.h, v_dc[j])
                    b = wi - 1
                    poff = 0 if b == 0 else 256
                    cur = ctxn[j % 2]
                    prev = ctxn[(j - 1) % 2]
                    gpsimd.tensor_sub(tmp2g[:, 0:256], cur[:, 0:256], prev[:, poff:poff + 256])
                    gpsimd.tensor_mul(tmp2g[:, 0:256], tmp2g[:, 0:256], fbC[:, b * 256:(b + 1) * 256])
                    ins = gpsimd.tensor_add(bigA[:, h * 1024 + b * 256:h * 1024 + (b + 1) * 256],
                                            tmp2g[:, 0:256], prev[:, poff:poff + 256])
                    assert sP.inc(ins) == bP + nblend[j]
        es5.close()

        es6s = ExitStack()
        psum_s2 = es6s.enter_context(nc.psum_tensor("psum_s2", [128, OWN], F32))
        psum_q2 = es6s.enter_context(nc.psum_tensor("psum_q2", [128, OWN], F32))
        es6 = ExitStack()
        psum_wo = [es6.enter_context(nc.psum_tensor(f"psum_wo{i}", [128, OWN], F32)) for i in range(2)]

        bT, bC, bV = sT.v, sC.v, sV.v
        bS1 = sS1.v
        t_mm6 = {}
        t_st6 = {}
        _tt = bT
        for oc in range(32):
            _tt += 1; t_mm6[oc] = _tt
            if oc >= 1:
                _tt += 1; t_st6[oc - 1] = _tt
        _tt += 1; t_st6[31] = _tt
        v_x1 = lambda oc: bV + oc + 1
        c_sb6 = lambda oc: bC + oc + 1
        bGs = [s.v for s in sG]
        g_after_p6 = []
        gtmp = list(bGs)
        for j in range(64):
            gtmp[j % 3] += 16
            g_after_p6.append(gtmp[j % 3])
        bXa, bXb = sXa.v, sXb.v
        x_after = lambda oc: (bXa if oc % 2 == 0 else bXb) + 16 * (oc // 2 + 1)

        with nc.named_scope("p6"), nc.Block("p6") as blk:
            @blk.gpsimd
            def _(gpsimd):
                for j in range(64):
                    if j >= 3:
                        gpsimd.wait_ge(sT.h, t_mm6[(j - 3) // 2])
                    slot = j % 3
                    oc, hf = j // 2, j % 2
                    ins = gpsimd.dma_start(out=wbf[:, slot * 2048:(slot + 1) * 2048],
                                           in_=wo[oc, :, hf * 2048:(hf + 1) * 2048])
                    assert sG[slot].inc(ins, 16) == g_after_p6[j]

            @blk.sync
            def _(sync):
                for oc in range(2):
                    sem = sXa if oc % 2 == 0 else sXb
                    fb = fbA if oc % 2 == 0 else fbB
                    assert sem.inc(sync.dma_start(out=fb[:, 0:1024],
                                                  in_=xT[oc * 128:(oc + 1) * 128, 256:1280]), 16) == x_after(oc)
                for oc in range(32):
                    sync.wait_ge(sV.h, v_x1(oc))
                    sS1.inc(sync.dma_start(out=x1scr[oc * 128:(oc + 1) * 128, :],
                                           in_=(fbC if oc % 2 == 0 else fbD)[:, :]), 16)
                    if oc + 2 < 32:
                        oc2 = oc + 2
                        sem = sXa if oc2 % 2 == 0 else sXb
                        fb = fbA if oc2 % 2 == 0 else fbB
                        assert sem.inc(sync.dma_start(out=fb[:, 0:1024],
                                                      in_=xT[oc2 * 128:(oc2 + 1) * 128, 256:1280]), 16) == x_after(oc2)

            @blk.tensor
            def _(tensor):
                def st_mms(j):
                    tensor.wait_ge(sC.h, c_sb6(j))
                    st_, sp_ = j == 0, j == 31
                    so = (j % 2) * 1024
                    x1b = bigB[:, j * 1024:(j + 1) * 1024]
                    tensor.matmul(psum_s2[0:1, 0:512], ones_c[:, :], x1b[:, 0:512], start=st_, stop=sp_)
                    tensor.matmul(psum_s2[0:1, 512:1024], ones_c[:, :], x1b[:, 512:1024], start=st_, stop=sp_)
                    tensor.matmul(psum_q2[0:1, 0:512], ones_c[:, :], sgbuf[:, so:so + 512], start=st_, stop=sp_)
                    ins = tensor.matmul(psum_q2[0:1, 512:1024], ones_c[:, :], sgbuf[:, so + 512:so + 1024], start=st_, stop=sp_)
                    assert sT.inc(ins) == t_st6[j]
                for oc in range(32):
                    if oc >= 2:
                        tensor.wait_ge(sV.h, v_x1(oc - 2))
                    pw = psum_wo[oc % 2]
                    for fs in range(NF):
                        if fs == 0:
                            tensor.wait_ge(sG[(2 * oc) % 3].h, g_after_p6[2 * oc])
                        if fs == 16:
                            tensor.wait_ge(sG[(2 * oc + 1) % 3].h, g_after_p6[2 * oc + 1])
                        slot = (2 * oc + fs // 16) % 3
                        lhs = wbf[:, slot * 2048 + (fs % 16) * 128:slot * 2048 + (fs % 16 + 1) * 128]
                        st_, sp_ = fs == 0, fs == NF - 1
                        tensor.matmul(pw[:, 0:512], lhs, bigA[:, fs * 1024:fs * 1024 + 512], start=st_, stop=sp_)
                        ins = tensor.matmul(pw[:, 512:1024], lhs, bigA[:, fs * 1024 + 512:fs * 1024 + 1024], start=st_, stop=sp_)
                    assert sT.inc(ins) == t_mm6[oc]
                    if oc >= 1:
                        st_mms(oc - 1)
                st_mms(31)

            @blk.vector
            def _(vector):
                for oc in range(32):
                    vector.wait_ge(sT.h, t_mm6[oc])
                    vector.wait_ge((sXa if oc % 2 == 0 else sXb).h, x_after(oc))
                    if oc >= 2:
                        vector.wait_ge(sS1.h, bS1 + 16 * (oc - 1))
                        vector.wait_ge(sC.h, c_sb6(oc - 2))
                    fb = fbA if oc % 2 == 0 else fbB
                    x1f = fbC if oc % 2 == 0 else fbD
                    ins = vector.tensor_add(x1f[:, :], psum_wo[oc % 2][:, :], fb[:, 0:1024])
                    assert sV.inc(ins) == v_x1(oc)

            @blk.scalar
            def _(scalar):
                for oc in range(32):
                    scalar.wait_ge(sV.h, v_x1(oc))
                    if oc >= 2:
                        scalar.wait_ge(sT.h, t_st6[oc - 2])
                    x1f = fbC if oc % 2 == 0 else fbD
                    so = (oc % 2) * 1024
                    scalar.copy(out=bigB[:, oc * 1024:(oc + 1) * 1024], in_=x1f[:, :])
                    ins = scalar.activation(out=sgbuf[:, so:so + 1024], in_=x1f[:, :], func=AF.Square)
                    assert sC.inc(ins) == c_sb6(oc)
        es6.close()

        m2 = fbA[0:1, 0:OWN]
        w2 = fbB[0:1, 0:OWN]
        hi2 = sgbuf[0:1, 0:OWN]
        lo2 = sgbuf[0:1, 1024:2048]
        ln_coef_block("p7a", psum_s2, psum_q2, OWN, m2, w2, hi2, lo2)
        es6s.close()

        bcast_block("p7b", OWN, m2, hi2, lo2)

        es7c = ExitStack()
        psum_r = es7c.enter_context(nc.psum_tensor("psum_r", [8, 1024], F32))
        psum_tr = es7c.enter_context(nc.psum_tensor("psum_tr", [128, 8], F32))
        psum_tw = es7c.enter_context(nc.psum_tensor("psum_tw", [8, 128], BF16))
        psum_we = es7c.enter_context(nc.psum_tensor("psum_we", [128, 1024], F32))
        psum_h = es7c.enter_context(nc.psum_tensor("psum_h", [128, 1024], F32))

        bT, bC, bV, bP7 = sT.v, sC.v, sV.v, sP.v
        vnorm7 = {}
        gnorm7 = {}
        _vv = bV
        for fs in range(NF):
            if fs not in GPSET:
                _vv += 1; vnorm7[fs] = _vv
        nvn7 = _vv
        _gg = bP7
        for fs in sorted(GPSET):
            _gg += 1; gnorm7[fs] = _gg
        t_r = bT + 1
        t_tr = lambda qt: bT + 2 + qt
        t_tw = lambda qt: bT + 10 + qt
        t_we = bT + 18
        t_h = bT + 19
        c_lg = bC + 1
        c_z = lambda qt: bC + 2 + qt
        c_wsb = lambda qt: bC + 10 + qt
        c_we = bC + 18
        v_w = lambda qt: nvn7 + qt + 1

        wALL7 = sW.v

        with nc.named_scope("p7c"), nc.Block("p7c") as blk:
            @blk.gpsimd
            def _(gpsimd):
                sG[0].inc(gpsimd.dma_start(out=wbf[:, 0:4096], in_=apk[:, :]), 16)
                for fs in sorted(GPSET):
                    a0 = fs * 1024
                    for hh in range(2):
                        o = a0 + hh * 512
                        gpsimd.tensor_mul(tmp2g[:, :], bigB[:, o:o + 512], a_sb[:, hh * 512:(hh + 1) * 512])
                        ins = gpsimd.tensor_add(bigB[:, o:o + 512], tmp2g[:, :], b_sb[:, hh * 512:(hh + 1) * 512])
                    assert sP.inc(ins) == gnorm7[fs]
            gAP = sG[0].v

            @blk.vector
            def _(vector):
                for fs in range(NF):
                    if fs in GPSET:
                        continue
                    a0 = fs * 1024
                    vector.tensor_mul(tmp[:, 0:1024], bigB[:, a0:a0 + 1024], a_sb[:, :])
                    ins = vector.tensor_add(bigB[:, a0:a0 + 1024], tmp[:, 0:1024], b_sb[:, :])
                    assert sV.inc(ins) == vnorm7[fs]

            @blk.tensor
            def _(tensor):
                tensor.wait_ge(sW.h, wALL7)
                for fs in range(NF):
                    st_, sp_ = fs == 0, fs == NF - 1
                    if fs in GPSET:
                        tensor.wait_ge(sP.h, gnorm7[fs])
                    else:
                        tensor.wait_ge(sV.h, vnorm7[fs])
                    lhs = wr_b[:, fs * 8:(fs + 1) * 8]
                    tensor.matmul(psum_r[0:8, 0:512], lhs, bigB[:, fs * 1024:fs * 1024 + 512], start=st_, stop=sp_)
                    ins = tensor.matmul(psum_r[0:8, 512:1024], lhs, bigB[:, fs * 1024 + 512:fs * 1024 + 1024], start=st_, stop=sp_)
                assert sT.inc(ins) == t_r
                tensor.wait_ge(sC.h, c_lg)
                for qt in range(8):
                    if qt >= 1:
                        tensor.wait_ge(sC.h, c_z(qt - 1))
                    ins = tensor.transpose(psum_tr[:, :], lg_sb[0:8, qt * 128:(qt + 1) * 128], idf8_s[:, :])
                    assert sT.inc(ins) == t_tr(qt)
                for qt in range(8):
                    tensor.wait_ge(sV.h, v_w(qt))
                    if qt >= 1:
                        tensor.wait_ge(sC.h, c_wsb(qt - 1))
                    ins = tensor.transpose(psum_tw[:, :], w_tm[:, qt * 8:(qt + 1) * 8], identb_s[:, :])
                    assert sT.inc(ins) == t_tw(qt)
                tensor.wait_ge(sC.h, c_wsb(7))
                tensor.matmul(psum_we[:, 0:512], eexp_s[:, :], w_sb[0:8, 0:512], start=True, stop=True)
                ins = tensor.matmul(psum_we[:, 512:1024], eexp_s[:, :], w_sb[0:8, 512:1024], start=True, stop=True)
                assert sT.inc(ins) == t_we
                tensor.wait_ge(sG[0].h, gAP)
                for fs in range(NF):
                    st_, sp_ = fs == 0, fs == NF - 1
                    lhs = wbf[:, fs * 128:(fs + 1) * 128]
                    tensor.matmul(psum_h[:, 0:512], lhs, bigB[:, fs * 1024:fs * 1024 + 512], start=st_, stop=sp_)
                    ins = tensor.matmul(psum_h[:, 512:1024], lhs, bigB[:, fs * 1024 + 512:fs * 1024 + 1024], start=st_, stop=sp_)
                assert sT.inc(ins) == t_h

            @blk.scalar
            def _(scalar):
                scalar.wait_ge(sT.h, t_r)
                ins = scalar.activation(out=lg_sb[0:8, 0:1024], in_=psum_r[0:8, :], func=AF.Identity, bias=br_s[0:8, 0:1])
                assert sC.inc(ins) == c_lg
                for qt in range(8):
                    scalar.wait_ge(sT.h, t_tr(qt))
                    ins = scalar.copy(out=z_sb[:, qt * 8:(qt + 1) * 8], in_=psum_tr[:, :])
                    assert sC.inc(ins) == c_z(qt)
                for qt in range(8):
                    scalar.wait_ge(sT.h, t_tw(qt))
                    ins = scalar.copy(out=w_sb[0:8, qt * 128:(qt + 1) * 128], in_=psum_tw[:, :])
                    assert sC.inc(ins) == c_wsb(qt)
                scalar.wait_ge(sT.h, t_we)
                ins = scalar.copy(out=tmp[:, 0:1024], in_=psum_we[:, :])
                assert sC.inc(ins) == c_we

            @blk.vector
            def _(vector):
                for qt in range(8):
                    vector.wait_ge(sC.h, c_z(qt))
                    z = z_sb[:, qt * 8:(qt + 1) * 8]
                    vector.memset(mk_sb[:, :], 1.0)
                    for it in range(8):
                        vector.tensor_mul(z8[:, :], z, mk_sb[:, :])
                        vector.reduce_sum(col_s[:, :], z8[:, :], axis=AX)
                        vector.reduce_sum(col_c[:, :], mk_sb[:, :], axis=AX)
                        vector.tensor_scalar_add(col_s[:, :], col_s[:, :], -1.0)
                        vector.reciprocal(col_c[:, :], col_c[:, :])
                        vector.tensor_mul(col_t[:, :], col_s[:, :], col_c[:, :])
                        if it < 7:
                            vector.tensor_single_scalar(mk_sb[:, :], z, col_t[:, :], OP.is_gt)
                    ins = vector.tensor_scalar(out=w_tm[:, qt * 8:(qt + 1) * 8], in0=z,
                                               scalar1=col_t[:, :], scalar2=0.0,
                                               op0=OP.subtract, op1=OP.max)
                    assert sV.inc(ins) == v_w(qt)
                vector.wait_ge(sT.h, t_h)
                vector.wait_ge(sC.h, c_we)
                sV.inc(vector.tensor_mul(u_bf[:, :], psum_h[:, :], tmp[:, 0:1024]))
        es7c.close()

        es8 = ExitStack()
        psum_o = [es8.enter_context(nc.psum_tensor(f"psum_o{i}", [128, OWN], F32)) for i in range(2)]

        bT, bV = sT.v, sV.v
        bS1_tot, bS2 = sS1.v, sS2.v
        t_mm8 = lambda oc: bT + oc + 1
        v_out = lambda oc: bV + oc + 1
        bGs = [s.v for s in sG]
        bG2 = sG[2].v
        g_after_p8 = []
        gtmp = [bGs[0], bGs[1], bGs[2] + 16]
        for j in range(64):
            gtmp[j % 3] += 16
            g_after_p8.append(gtmp[j % 3])
        bXa, bXb = sXa.v, sXb.v
        x_after8 = lambda oc: (bXa if oc % 2 == 0 else bXb) + 16 * (oc // 2 + 1)

        wALL8 = sW.v

        with nc.named_scope("p8"), nc.Block("p8") as blk:
            @blk.gpsimd
            def _(gpsimd):
                sG[2].inc(gpsimd.dma_start(out=sgbuf[:, :], in_=bpk[:, :]), 16)
                for j in range(64):
                    if j >= 3:
                        gpsimd.wait_ge(sT.h, t_mm8((j - 3) // 2))
                    slot = j % 3
                    oc, hf = j // 2, j % 2
                    ins = gpsimd.dma_start(out=wbf[:, slot * 2048:(slot + 1) * 2048],
                                           in_=wb[oc, :, hf * 2048:(hf + 1) * 2048])
                    assert sG[slot].inc(ins, 16) == g_after_p8[j]

            @blk.sync
            def _(sync):
                sync.wait_ge(sS1.h, bS1_tot)
                for oc in range(2):
                    sem = sXa if oc % 2 == 0 else sXb
                    fb = fbA if oc % 2 == 0 else fbB
                    assert sem.inc(sync.dma_start(out=fb[:, 0:1024],
                                                  in_=x1scr[oc * 128:(oc + 1) * 128, :]), 16) == x_after8(oc)
                for oc in range(32):
                    sync.wait_ge(sV.h, v_out(oc))
                    sS2.inc(sync.dma_start(out=outT[oc * 128:(oc + 1) * 128, :],
                                           in_=(fbC if oc % 2 == 0 else fbD)[:, :]), 16)
                    if oc + 2 < 32:
                        oc2 = oc + 2
                        sem = sXa if oc2 % 2 == 0 else sXb
                        fb = fbA if oc2 % 2 == 0 else fbB
                        assert sem.inc(sync.dma_start(out=fb[:, 0:1024],
                                                      in_=x1scr[oc2 * 128:(oc2 + 1) * 128, :]), 16) == x_after8(oc2)
                sync.wait_ge(sS2.h, bS2 + 16 * 32)

            @blk.tensor
            def _(tensor):
                tensor.wait_ge(sG[2].h, bG2 + 16)
                for oc in range(32):
                    if oc >= 2:
                        tensor.wait_ge(sV.h, v_out(oc - 2))
                    po = psum_o[oc % 2]
                    for fs in range(NF):
                        if fs == 0:
                            tensor.wait_ge(sG[(2 * oc) % 3].h, g_after_p8[2 * oc])
                        if fs == 16:
                            tensor.wait_ge(sG[(2 * oc + 1) % 3].h, g_after_p8[2 * oc + 1])
                        slot = (2 * oc + fs // 16) % 3
                        lhs = wbf[:, slot * 2048 + (fs % 16) * 128:slot * 2048 + (fs % 16 + 1) * 128]
                        st_ = fs == 0
                        tensor.matmul(po[:, 0:512], lhs, bigB[:, fs * 1024:fs * 1024 + 512], start=st_, stop=False)
                        tensor.matmul(po[:, 512:1024], lhs, bigB[:, fs * 1024 + 512:fs * 1024 + 1024], start=st_, stop=False)
                    blh = sgbuf[:, oc * 128:(oc + 1) * 128]
                    tensor.matmul(po[:, 0:512], blh, u_bf[:, 0:512], start=False, stop=True)
                    ins = tensor.matmul(po[:, 512:1024], blh, u_bf[:, 512:1024], start=False, stop=True)
                    assert sT.inc(ins) == t_mm8(oc)

            @blk.vector
            def _(vector):
                for oc in range(32):
                    if oc == 0:
                        vector.wait_ge(sW.h, wALL8)
                    vector.wait_ge(sT.h, t_mm8(oc))
                    vector.wait_ge((sXa if oc % 2 == 0 else sXb).h, x_after8(oc))
                    if oc >= 2:
                        vector.wait_ge(sS2.h, bS2 + 16 * (oc - 1))
                    fb = fbA if oc % 2 == 0 else fbB
                    outf = fbC if oc % 2 == 0 else fbD
                    ins = vector.scalar_tensor_tensor(outf[:, :], psum_o[oc % 2][:, :],
                                                      bb_s[:, oc:oc + 1], fb[:, 0:1024],
                                                      OP.add, OP.add)
                    assert sV.inc(ins) == v_out(oc)
        es8.close()

    return nc


def _rope_tables(dt):
    inv = 1.0 / (10000.0 ** (np.arange(0, DH, 2, dtype=np.float64) / DH))
    t = np.arange(S, dtype=np.float64)
    f = np.outer(t, inv)
    cos = np.zeros((S, DH), np.float64)
    sin = np.zeros((S, DH), np.float64)
    cos[:, 0::2] = np.cos(f); cos[:, 1::2] = np.cos(f)
    sin[:, 0::2] = np.sin(f); sin[:, 1::2] = np.sin(f)
    return cos.astype(dt), sin.astype(dt)


def kernel(L0_attn_x, L0_attn_Wq, L0_attn_Wk, L0_attn_Wv, L0_attn_Wo,
           L0_ffn_W_base, L0_ffn_b_base, L0_ffn_W_router, L0_ffn_b_router,
           L0_ffn_lora_A, L0_ffn_lora_B):
    x = np.asarray(L0_attn_x)
    dt = x.dtype
    bf = ml_dtypes.bfloat16

    def lay_big(W):
        return np.ascontiguousarray(
            np.asarray(W).reshape(32, 128, 32, 128).transpose(0, 3, 2, 1).reshape(32, 128, 4096))

    wq_l = lay_big(L0_attn_Wq)
    wo_l = lay_big(L0_attn_Wo)
    wb_l = lay_big(L0_ffn_W_base)
    wk_l = np.ascontiguousarray(np.asarray(L0_attn_Wk).reshape(128, 32, 128).transpose(2, 1, 0).reshape(128, 4096))
    wv_l = np.ascontiguousarray(np.asarray(L0_attn_Wv).reshape(128, 32, 128).transpose(2, 1, 0).reshape(128, 4096))
    wr_l = np.ascontiguousarray(np.asarray(L0_ffn_W_router).reshape(8, 32, 128).transpose(2, 1, 0).reshape(128, 256))
    apk_l = np.ascontiguousarray(np.asarray(L0_ffn_lora_A).transpose(1, 0, 2).reshape(4096, 128)
                                 .reshape(32, 128, 128).transpose(1, 0, 2).reshape(128, 4096))
    bpk_l = np.ascontiguousarray(np.asarray(L0_ffn_lora_B).reshape(128, 4096))
    bb_l = np.ascontiguousarray(np.asarray(L0_ffn_b_base).reshape(32, 128).T)
    br_l = np.ascontiguousarray(np.asarray(L0_ffn_b_router).reshape(8, 1))

    cosS, sinS = _rope_tables(dt)
    SCALE = dt.type(1.0) / np.sqrt(dt.type(DH))

    identb = np.eye(128).astype(bf)
    idf8 = np.eye(8).astype(dt)
    P = np.zeros((128, 128), np.float64)
    for i in range(64):
        P[i, i + 64] = -1.0
        P[i + 64, i] = 1.0
    ptr_l = np.ascontiguousarray(P.T).astype(bf)
    eexp_l = np.zeros((8, 128), np.float64)
    for e in range(8):
        eexp_l[e, e * 16:(e + 1) * 16] = 1.0
    eexp_l = eexp_l.astype(bf)

    xT_full = np.ascontiguousarray(x.reshape(S, D).T)
    base_alpha = np.tile(np.linspace(0.0, 1.0, 256, dtype=dt), 4)

    in_maps = []
    for c in range(NC):
        lo, hi = c * OWN, (c + 1) * OWN
        xTc = np.zeros((D, HALO), dt)
        s0, s1 = max(lo - 256, 0), min(hi + 256, S)
        xTc[:, s0 - (lo - 256):s1 - (lo - 256)] = xT_full[:, s0:s1]
        mask = np.full(HALO, -40.0, np.float64)
        mask[s0 - (lo - 256):s1 - (lo - 256)] = 0.0
        maskk_c = np.ascontiguousarray(mask.reshape(12, 128).T).astype(dt)
        al = base_alpha.copy()
        if c == 0:
            al[:256] = 1.0
        alpha_c = np.ascontiguousarray(np.broadcast_to(al, (128, OWN)))
        pos_k = np.clip(np.arange(lo - 256, hi + 256), 0, S - 1)
        in_maps.append({
            "xT": xTc,
            "wq": wq_l, "wo": wo_l, "wb": wb_l, "wk": wk_l, "wv": wv_l,
            "wr": wr_l, "apk": apk_l, "bpk": bpk_l, "bb": bb_l, "br": br_l,
            "cosq": np.ascontiguousarray((cosS[lo:hi] * SCALE).T),
            "sinq": np.ascontiguousarray((sinS[lo:hi] * SCALE).T),
            "cosk": np.ascontiguousarray(cosS[pos_k].T),
            "sink": np.ascontiguousarray(sinS[pos_k].T),
            "maskk": maskk_c, "alpha": alpha_c,
            "identb": identb, "idf8": idf8, "ptr": ptr_l, "eexp": eexp_l,
        })

    nc = build_nc()
    res = run_bass_kernel_spmd(nc, in_maps, core_ids=list(range(NC)))
    outs = [res.results[c]["outT"] for c in range(NC)]
    full = np.concatenate(outs, axis=1)
    return np.ascontiguousarray(full.T).reshape(1, S, D)


# version 08
# speedup vs baseline: 1.0122x; 1.0041x over previous
import numpy as np
import ml_dtypes
from contextlib import ExitStack

import concourse.bass as bass
import concourse.mybir as mybir
from concourse.bass_utils import run_bass_kernel_spmd

F32 = mybir.dt.float32
BF16 = mybir.dt.bfloat16
OP = mybir.AluOpType
AF = mybir.ActivationFunctionType
AX = mybir.AxisListType.X

S, D, NC, OWN, HALO = 8192, 4096, 8, 1024, 1536
H, DH, NF = 32, 128, 32
EPS = 1e-5
WINS = [(0, 0, 256), (256, 0, 512), (512, 256, 512), (768, 512, 512), (1024, 768, 256)]


class Sem:
    def __init__(self, ctx, nc, name):
        self.h = ctx.enter_context(nc.semaphore(name))
        self.v = 0

    def inc(self, instr, n=1):
        instr.then_inc(self.h, n)
        self.v += n
        return self.v


def build_nc():
    nc = bass.Bass("TRN2", target_bir_lowering=False, detect_race_conditions=False)

    xT = nc.dram_tensor("xT", [D, HALO], F32, kind="ExternalInput")
    wq = nc.dram_tensor("wq", [32, 128, 4096], F32, kind="ExternalInput")
    wo = nc.dram_tensor("wo", [32, 128, 4096], F32, kind="ExternalInput")
    wb = nc.dram_tensor("wb", [32, 128, 4096], F32, kind="ExternalInput")
    wk = nc.dram_tensor("wk", [128, 4096], F32, kind="ExternalInput")
    wv = nc.dram_tensor("wv", [128, 4096], F32, kind="ExternalInput")
    wr = nc.dram_tensor("wr", [128, 256], F32, kind="ExternalInput")
    apk = nc.dram_tensor("apk", [128, 4096], F32, kind="ExternalInput")
    bpk = nc.dram_tensor("bpk", [128, 4096], F32, kind="ExternalInput")
    bb = nc.dram_tensor("bb", [128, 32], F32, kind="ExternalInput")
    br = nc.dram_tensor("br", [8, 1], F32, kind="ExternalInput")
    cosq = nc.dram_tensor("cosq", [128, OWN], F32, kind="ExternalInput")
    sinq = nc.dram_tensor("sinq", [128, OWN], F32, kind="ExternalInput")
    cosk = nc.dram_tensor("cosk", [128, HALO], F32, kind="ExternalInput")
    sink = nc.dram_tensor("sink", [128, HALO], F32, kind="ExternalInput")
    maskk = nc.dram_tensor("maskk", [128, 12], F32, kind="ExternalInput")
    alpha = nc.dram_tensor("alpha", [128, OWN], F32, kind="ExternalInput")
    identb = nc.dram_tensor("identb", [128, 128], BF16, kind="ExternalInput")
    idf8 = nc.dram_tensor("idf8", [8, 8], F32, kind="ExternalInput")
    ptr = nc.dram_tensor("ptr", [128, 128], BF16, kind="ExternalInput")
    eexp = nc.dram_tensor("eexp", [8, 128], BF16, kind="ExternalInput")
    x1scr = nc.dram_tensor("x1scr", [D, OWN], F32)
    outT = nc.dram_tensor("outT", [D, OWN], F32, kind="ExternalOutput")

    ctx = ExitStack()
    with ctx:
        sb = lambda n, s, d: ctx.enter_context(nc.sbuf_tensor(n, s, d))
        bigA = sb("bigA", [128, 32 * 1024], BF16)
        bigB = sb("bigB", [128, 32 * 1024], BF16)
        sgbuf = sb("sgbuf", [128, 4096], BF16)
        wbf = sb("wbf", [128, 3 * 2048], BF16)
        fbA = sb("fbA", [128, HALO], F32)
        fbB = sb("fbB", [128, HALO], F32)
        fbC = sb("fbC", [128, OWN], F32)
        fbD = sb("fbD", [128, OWN], F32)
        kr_s = sb("kr_s", [128, HALO], BF16)
        ku_s = sb("ku_s", [128, HALO], BF16)
        vtm_s = sb("vtm_s", [128, HALO], BF16)
        tmp = sb("tmp", [128, HALO], F32)
        ctxn = [sb("ctxn0", [128, 512], F32), sb("ctxn1", [128, 512], F32)]
        u_bf = sb("u_bf", [128, OWN], BF16)
        wr_b = sb("wr_b", [128, 256], BF16)
        tmp2g = sb("tmp2g", [128, 512], F32)
        a_sb = sb("a_sb", [128, 1024], F32)
        b_sb = sb("b_sb", [128, 1024], F32)
        rd_hl = [sb("rd_h0", [1, 512], BF16), sb("rd_h1", [1, 512], BF16)]
        z_sb = sb("z_sb", [128, 64], F32)
        z8 = sb("z8", [128, 8], F32)
        mk_sb = sb("mk_sb", [128, 8], F32)
        col_s = sb("col_s", [128, 1], F32)
        col_c = sb("col_c", [128, 1], F32)
        col_t = sb("col_t", [128, 1], F32)
        w_tm = sb("w_tm", [128, 64], BF16)
        w_sb = sb("w_sb", [8, 1024], BF16)
        identb_s = sb("identb_s", [128, 128], BF16)
        idf8_s = sb("idf8_s", [8, 8], F32)
        ptr_s = sb("ptr_s", [128, 128], BF16)
        eexp_s = sb("eexp_s", [8, 128], BF16)
        ones_c = sb("ones_c", [128, 1], BF16)
        ones_r = sb("ones_r", [1, 128], BF16)
        maskk_s = sb("maskk_s", [128, 12], F32)
        bb_s = sb("bb_s", [128, 32], F32)
        br_s = sb("br_s", [8, 1], F32)
        eps_s = sb("eps_s", [1, 1], F32)
        lg_sb = fbA
        assert nc.sbuf_bytes_remaining > 512, nc.sbuf_bytes_remaining

        EDG = 0
        WKO = 16 * 1024
        WVO = 20 * 1024

        sW = Sem(ctx, nc, "sW")
        sXa = Sem(ctx, nc, "sXa")
        sXb = Sem(ctx, nc, "sXb")
        sG = [Sem(ctx, nc, f"sG{i}") for i in range(3)]
        sS1 = Sem(ctx, nc, "sS1")
        sS2 = Sem(ctx, nc, "sS2")
        sP = Sem(ctx, nc, "sP")
        sT = Sem(ctx, nc, "sT")
        sV = Sem(ctx, nc, "sV")
        sC = Sem(ctx, nc, "sC")

        es1 = ExitStack()
        psum_sum = es1.enter_context(nc.psum_tensor("psum_sum", [128, HALO], F32))
        psum_sq = es1.enter_context(nc.psum_tensor("psum_sq", [128, HALO], F32))

        bT, bC = sT.v, sC.v
        grp_sems = [sG[0], sG[1], sG[2], sXa]
        grp_after = [sem.v + 48 for sem in grp_sems]
        bXB = sXb.v
        sc_after = lambda fs: bC + fs + 1
        tt_after = lambda fs: bT + fs + 1

        with nc.named_scope("p1"), nc.Block("p1") as blk:
            @blk.sync
            def _(sync):
                for dst, src in ((identb_s, identb), (idf8_s, idf8), (ptr_s, ptr),
                                 (eexp_s, eexp), (maskk_s, maskk), (bb_s, bb),
                                 (br_s, br)):
                    sW.inc(sync.dma_start(out=dst[:, :], in_=src[:, :]), 16)

            @blk.vector
            def _(vector):
                vector.memset(ones_c[:, :], 1.0)
                vector.memset(eps_s[:, :], EPS)
                sV.inc(vector.memset(ones_r[:, :], 1.0))
            vONES = sV.v

            @blk.gpsimd
            def _(gpsimd):
                for g in range(4):
                    own_src = bass.AP(xT, g * 1024 * HALO + 256,
                                      [[HALO, 128], [128 * HALO, 8], [1, 1024]])
                    own_dst = bass.AP(bigA, g * 8192,
                                      [[32 * 1024, 128], [1024, 8], [1, 1024]])
                    grp_sems[g].inc(gpsimd.dma_start(out=own_dst, in_=own_src), 16)
                    for side in range(2):
                        edg_src = bass.AP(xT, g * 1024 * HALO + side * 1280,
                                          [[HALO, 128], [128 * HALO, 8], [1, 256]])
                        edg_dst = bass.AP(bigB, EDG + g * 4096 + side * 256,
                                          [[32 * 1024, 128], [512, 8], [1, 256]])
                        grp_sems[g].inc(gpsimd.dma_start(out=edg_dst, in_=edg_src), 16)
                sXb.inc(gpsimd.dma_start(out=wr_b[:, :], in_=wr[:, :]), 16)
                sXb.inc(gpsimd.dma_start(out=bigB[:, WKO:WKO + 4096], in_=wk[:, :]), 16)
                sXb.inc(gpsimd.dma_start(out=bigB[:, WVO:WVO + 4096], in_=wv[:, :]), 16)

            @blk.scalar
            def _(scalar):
                for fs in range(NF):
                    if fs % 8 == 0:
                        scalar.wait_ge(grp_sems[fs // 8].h, grp_after[fs // 8])
                    if fs >= 2:
                        scalar.wait_ge(sT.h, tt_after(fs - 2))
                    so = (fs % 2) * 2048
                    scalar.activation(out=sgbuf[:, so:so + 1024],
                                      in_=bigA[:, fs * 1024:(fs + 1) * 1024], func=AF.Square)
                    ins = scalar.activation(out=sgbuf[:, so + 1024:so + 1536],
                                            in_=bigB[:, EDG + fs * 512:EDG + fs * 512 + 512],
                                            func=AF.Square)
                    assert sC.inc(ins) == sc_after(fs)

            @blk.tensor
            def _(tensor):
                for fs in range(NF):
                    st_, sp_ = fs == 0, fs == NF - 1
                    if fs == 0:
                        tensor.wait_ge(sV.h, vONES)
                    if fs % 8 == 0:
                        tensor.wait_ge(grp_sems[fs // 8].h, grp_after[fs // 8])
                    tensor.wait_ge(sC.h, sc_after(fs))
                    so = (fs % 2) * 2048
                    a0 = fs * 1024
                    e0 = EDG + fs * 512
                    tensor.matmul(psum_sum[0:1, 0:512], ones_c[:, :], bigA[:, a0:a0 + 512], start=st_, stop=sp_)
                    tensor.matmul(psum_sum[0:1, 512:1024], ones_c[:, :], bigA[:, a0 + 512:a0 + 1024], start=st_, stop=sp_)
                    tensor.matmul(psum_sum[0:1, 1024:1536], ones_c[:, :], bigB[:, e0:e0 + 512], start=st_, stop=sp_)
                    tensor.matmul(psum_sq[0:1, 0:512], ones_c[:, :], sgbuf[:, so:so + 512], start=st_, stop=sp_)
                    tensor.matmul(psum_sq[0:1, 512:1024], ones_c[:, :], sgbuf[:, so + 512:so + 1024], start=st_, stop=sp_)
                    ins = tensor.matmul(psum_sq[0:1, 1024:1536], ones_c[:, :], sgbuf[:, so + 1024:so + 1536], start=st_, stop=sp_)
                    assert sT.inc(ins) == tt_after(fs)

        def ln_coef_block(name, psA, psB, n, m_row, w_row, hi_row, lo_row):
            with nc.named_scope(name), nc.Block(name) as blk2:
                @blk2.vector
                def _(vector):
                    vector.tensor_scalar_mul(m_row, psA[0:1, 0:n], 1.0 / D)
                    vector.scalar_tensor_tensor(w_row, m_row, -1.0, m_row, OP.mult, OP.mult)
                    ins = vector.scalar_tensor_tensor(w_row, psB[0:1, 0:n], 1.0 / D, w_row, OP.mult, OP.add)
                    sV.inc(ins)
                vS = sV.v

                @blk2.scalar
                def _(scalar):
                    scalar.wait_ge(sV.h, vS)
                    sC.inc(scalar.activation(out=w_row, in_=w_row, func=AF.Sqrt, bias=eps_s[0:1, 0:1]))
                cS = sC.v

                @blk2.vector
                def _(vector):
                    vector.wait_ge(sC.h, cS)
                    vector.reciprocal(w_row, w_row)
                    vector.tensor_mul(m_row, m_row, w_row)
                    vector.tensor_scalar_mul(m_row, m_row, -1.0)
                    vector.tensor_copy(out=hi_row, in_=w_row)
                    vector.tensor_sub(lo_row, w_row, hi_row)
                    sV.inc(vector.engine_nop())

        m1 = fbA[0:1, 0:HALO]
        w1 = fbB[0:1, 0:HALO]
        hi1 = sgbuf[0:1, 0:HALO]
        lo1 = sgbuf[0:1, 2048:2048 + HALO]
        ln_coef_block("p2a", psum_sum, psum_sq, HALO, m1, w1, hi1, lo1)
        es1.close()


        def bcast_block(name, n, m_row, hi_row, lo_row):
            es2o = ExitStack()
            psA = es2o.enter_context(nc.psum_tensor(name + "_pa", [128, 1024], F32))
            psB = es2o.enter_context(nc.psum_tensor(name + "_pb", [128, 1024], F32))
            if n > 1024:
                edge_ps = (es2o.enter_context(nc.psum_tensor(name + "_pae", [128, 512], F32)),
                           es2o.enter_context(nc.psum_tensor(name + "_pbe", [128, 512], F32)))
            with nc.named_scope(name), nc.Block(name) as blk2:
                @blk2.tensor
                def _(tensor):
                    for c in range(0, n, 512):
                        w = min(512, n - c)
                        dst = psA[:, c:c + w] if c < 1024 else edge_ps[0][:, c - 1024:c - 1024 + w]
                        tensor.matmul(dst, ones_r[:, :], hi_row[0:1, c:c + w], start=True, stop=False)
                        ins = tensor.matmul(dst, ones_r[:, :], lo_row[0:1, c:c + w], start=False, stop=True)
                    sT.inc(ins)
                tS = sT.v

                @blk2.vector
                def _(vector):
                    vector.wait_ge(sT.h, tS)
                    vector.tensor_copy(out=hi_row, in_=m_row)
                    vector.tensor_sub(lo_row, m_row, hi_row)
                    sV.inc(vector.engine_nop())
                vS = sV.v

                @blk2.tensor
                def _(tensor):
                    tensor.wait_ge(sV.h, vS)
                    for c in range(0, n, 512):
                        w = min(512, n - c)
                        dst = psB[:, c:c + w] if c < 1024 else edge_ps[1][:, c - 1024:c - 1024 + w]
                        tensor.matmul(dst, ones_r[:, :], hi_row[0:1, c:c + w], start=True, stop=False)
                        ins = tensor.matmul(dst, ones_r[:, :], lo_row[0:1, c:c + w], start=False, stop=True)
                    sT.inc(ins)
                tS2 = sT.v

                @blk2.scalar
                def _(scalar):
                    scalar.wait_ge(sT.h, tS2)
                    scalar.copy(out=a_sb[:, :], in_=psA[:, :])
                    ins = scalar.copy(out=b_sb[:, :], in_=psB[:, :])
                    if n > 1024:
                        scalar.copy(out=ctxn[0][:, :], in_=edge_ps[0][:, :])
                        ins = scalar.copy(out=ctxn[1][:, :], in_=edge_ps[1][:, :])
                    sC.inc(ins)
            es2o.close()

        bcast_block("p2b", HALO, m1, hi1, lo1)

        GPSET = set(fs for fs in range(NF) if fs % 3 == 0)
        bV3, bP3 = sV.v, sP.v
        vnorm = {}
        gnorm = {}
        _vv = bV3
        for fs in range(NF):
            _vv += 1; vnorm[fs] = _vv
        _gg = bP3
        for fs in sorted(GPSET):
            _gg += 1; gnorm[fs] = _gg

        es3v = ExitStack()
        psum_v = es3v.enter_context(nc.psum_tensor("psum_v", [128, HALO], F32))
        es3 = ExitStack()
        psum_k = es3.enter_context(nc.psum_tensor("psum_k", [128, HALO], F32))

        with nc.named_scope("p3kv"), nc.Block("p3kv") as blk:
            @blk.sync
            def _(sync):
                sW.inc(sync.dma_start(out=fbA[:, :], in_=cosk[:, :]), 16)
                sW.inc(sync.dma_start(out=fbB[:, :], in_=sink[:, :]), 16)
            wCK = sW.v

            @blk.vector
            def _(vector):
                for fs in range(NF):
                    a0 = fs * 1024
                    e0 = EDG + fs * 512
                    if fs not in GPSET:
                        vector.tensor_mul(tmp[:, 0:1024], bigA[:, a0:a0 + 1024], a_sb[:, :])
                        vector.tensor_add(bigA[:, a0:a0 + 1024], tmp[:, 0:1024], b_sb[:, :])
                    vector.tensor_mul(tmp[:, 1024:1536], bigB[:, e0:e0 + 512], ctxn[0][:, :])
                    ins = vector.tensor_add(bigB[:, e0:e0 + 512], tmp[:, 1024:1536], ctxn[1][:, :])
                    assert sV.inc(ins) == vnorm[fs]

            @blk.gpsimd
            def _(gpsimd):
                for fs in sorted(GPSET):
                    a0 = fs * 1024
                    for hh in range(2):
                        o = a0 + hh * 512
                        gpsimd.tensor_mul(tmp2g[:, :], bigA[:, o:o + 512], a_sb[:, hh * 512:(hh + 1) * 512])
                        ins = gpsimd.tensor_add(bigA[:, o:o + 512], tmp2g[:, :], b_sb[:, hh * 512:(hh + 1) * 512])
                    assert sP.inc(ins) == gnorm[fs]

            @blk.tensor
            def _(tensor):
                tensor.wait_ge(sXb.h, bXB + 48)
                for fs in range(NF):
                    st_, sp_ = fs == 0, fs == NF - 1
                    a0 = fs * 1024
                    e0 = EDG + fs * 512
                    tensor.wait_ge(sV.h, vnorm[fs])
                    if fs in GPSET:
                        tensor.wait_ge(sP.h, gnorm[fs])
                    for ps, woff in ((psum_k, WKO), (psum_v, WVO)):
                        lhs = bigB[:, woff + fs * 128:woff + (fs + 1) * 128]
                        ps_ = ps
                        tensor.matmul(ps_[:, 0:512], lhs, bigA[:, a0:a0 + 512], start=st_, stop=sp_)
                        tensor.matmul(ps_[:, 512:1024], lhs, bigA[:, a0 + 512:a0 + 1024], start=st_, stop=sp_)
                        ins = tensor.matmul(ps_[:, 1024:1536], lhs, bigB[:, e0:e0 + 512], start=st_, stop=sp_)
                sT.inc(ins)
            tKV = sT.v

            @blk.scalar
            def _(scalar):
                scalar.wait_ge(sT.h, tKV)
                scalar.copy(out=ku_s[:, 256:1280], in_=psum_k[:, 0:1024])
                scalar.copy(out=ku_s[:, 0:256], in_=psum_k[:, 1024:1280])
                sC.inc(scalar.copy(out=ku_s[:, 1280:1536], in_=psum_k[:, 1280:1536]))
            cK = sC.v
        es3.close()

        es3r = ExitStack()
        psum_rot = es3r.enter_context(nc.psum_tensor("psum_rot", [128, HALO], F32))
        psum_t = [es3r.enter_context(nc.psum_tensor(f"psum_t{i}", [128, 128], BF16)) for i in range(2)]

        wALL3 = sW.v

        with nc.named_scope("p3r"), nc.Block("p3r") as blk:
            @blk.tensor
            def _(tensor):
                tensor.wait_ge(sW.h, wALL3)
                tensor.wait_ge(sC.h, cK)
                for c in range(0, HALO, 512):
                    ins = tensor.matmul(psum_rot[:, c:c + 512], ptr_s[:, :], ku_s[:, c:c + 512], start=True, stop=True)
                sT.inc(ins)
            tROT = sT.v

            @blk.vector
            def _(vector):
                vector.wait_ge(sW.h, wCK)
                vector.tensor_mul(tmp[:, :], ku_s[:, :], fbA[:, :])
                vector.wait_ge(sT.h, tROT)
                vector.tensor_tensor(psum_rot[:, :], psum_rot[:, :], fbB[:, :], op=OP.mult)
                vector.tensor_add(kr_s[:, :], tmp[:, :], psum_rot[:, :])
                sV.inc(vector.engine_nop())
            vKR = sV.v

            @blk.scalar
            def _(scalar):
                scalar.wait_ge(sV.h, vKR)
                scalar.copy(out=ku_s[:, 256:1280], in_=psum_v[:, 0:1024])
                scalar.copy(out=ku_s[:, 0:256], in_=psum_v[:, 1024:1280])
                sC.inc(scalar.copy(out=ku_s[:, 1280:1536], in_=psum_v[:, 1280:1536]))
            cV = sC.v

            cv_pre = [cV + 1 + j for j in range(12)]
            tv_pre = [tROT + 1 + j for j in range(12)]

            @blk.tensor
            def _(tensor):
                tensor.wait_ge(sC.h, cV)
                for j in range(12):
                    if j >= 2:
                        tensor.wait_ge(sC.h, cv_pre[j - 2])
                    ins = tensor.transpose(psum_t[j % 2][:, :], ku_s[:, j * 128:(j + 1) * 128], identb_s[:, :])
                    assert sT.inc(ins) == tv_pre[j]

            @blk.scalar
            def _(scalar):
                for j in range(12):
                    scalar.wait_ge(sT.h, tv_pre[j])
                    ins = scalar.copy(out=vtm_s[:, j * 128:(j + 1) * 128], in_=psum_t[j % 2][:, :])
                    assert sC.inc(ins) == cv_pre[j]
        es3r.close()
        es3v.close()

        es4 = ExitStack()
        psum_q = [es4.enter_context(nc.psum_tensor(f"psum_q{i}", [128, OWN], F32)) for i in range(2)]
        psum_rq = es4.enter_context(nc.psum_tensor("psum_rq", [128, OWN], F32))

        bT, bC, bV = sT.v, sC.v, sV.v
        t_q = {}
        t_rot = {}
        _tt = bT
        for h in range(H):
            _tt += 1; t_q[h] = _tt
            if h >= 1:
                _tt += 1; t_rot[h - 1] = _tt
        _tt += 1; t_rot[H - 1] = _tt
        c_u = lambda h: bC + h + 1
        v_qr = lambda h: bV + h + 1
        bGs = [s.v for s in sG]
        g_after_p4 = []
        gtmp = list(bGs)
        for j in range(64):
            gtmp[j % 3] += 16
            g_after_p4.append(gtmp[j % 3])

        with nc.named_scope("p4"), nc.Block("p4") as blk:
            @blk.sync
            def _(sync):
                sW.inc(sync.dma_start(out=fbC[:, :], in_=cosq[:, :]), 16)
                sW.inc(sync.dma_start(out=fbD[:, :], in_=sinq[:, :]), 16)
            wCQ = sW.v

            @blk.gpsimd
            def _(gpsimd):
                for j in range(64):
                    if j >= 3:
                        gpsimd.wait_ge(sT.h, t_q[(j - 3) // 2])
                    slot = j % 3
                    oc, hf = j // 2, j % 2
                    ins = gpsimd.dma_start(
                        out=wbf[:, slot * 2048:(slot + 1) * 2048],
                        in_=wq[oc, :, hf * 2048:(hf + 1) * 2048])
                    assert sG[slot].inc(ins, 16) == g_after_p4[j]

            @blk.tensor
            def _(tensor):
                def rot_mms(j):
                    tensor.wait_ge(sC.h, c_u(j))
                    if j >= 1:
                        tensor.wait_ge(sV.h, v_qr(j - 1))
                    tensor.matmul(psum_rq[:, 0:512], ptr_s[:, :], u_bf[:, 0:512], start=True, stop=True)
                    ins = tensor.matmul(psum_rq[:, 512:1024], ptr_s[:, :], u_bf[:, 512:1024], start=True, stop=True)
                    assert sT.inc(ins) == t_rot[j]
                for h in range(H):
                    if h >= 2:
                        tensor.wait_ge(sC.h, c_u(h - 2))
                    pq = psum_q[h % 2]
                    for fs in range(NF):
                        if fs == 0:
                            tensor.wait_ge(sG[(2 * h) % 3].h, g_after_p4[2 * h])
                        if fs == 16:
                            tensor.wait_ge(sG[(2 * h + 1) % 3].h, g_after_p4[2 * h + 1])
                        slot = (2 * h + fs // 16) % 3
                        lhs = wbf[:, slot * 2048 + (fs % 16) * 128:slot * 2048 + (fs % 16 + 1) * 128]
                        st_, sp_ = fs == 0, fs == NF - 1
                        tensor.matmul(pq[:, 0:512], lhs, bigA[:, fs * 1024:fs * 1024 + 512], start=st_, stop=sp_)
                        ins = tensor.matmul(pq[:, 512:1024], lhs, bigA[:, fs * 1024 + 512:fs * 1024 + 1024], start=st_, stop=sp_)
                    assert sT.inc(ins) == t_q[h]
                    if h >= 1:
                        rot_mms(h - 1)
                rot_mms(H - 1)

            @blk.scalar
            def _(scalar):
                for h in range(H):
                    scalar.wait_ge(sT.h, t_q[h])
                    if h >= 1:
                        scalar.wait_ge(sV.h, v_qr(h - 1))
                        scalar.wait_ge(sT.h, t_rot[h - 1])
                    ins = scalar.copy(out=u_bf[:, :], in_=psum_q[h % 2][:, :])
                    assert sC.inc(ins) == c_u(h)

            @blk.vector
            def _(vector):
                for h in range(H):
                    vector.wait_ge(sC.h, c_u(h))
                    if h == 0:
                        vector.wait_ge(sW.h, wCQ)
                    vector.tensor_mul(tmp[:, 0:1024], u_bf[:, :], fbC[:, :])
                    vector.wait_ge(sT.h, t_rot[h])
                    vector.tensor_tensor(psum_rq[:, :], psum_rq[:, :], fbD[:, :], op=OP.mult)
                    ins = vector.tensor_add(bigB[:, h * 1024:(h + 1) * 1024], tmp[:, 0:1024], psum_rq[:, :])
                    assert sV.inc(ins) == v_qr(h)
        es4.close()

        es5 = ExitStack()
        psum_Sb = es5.enter_context(nc.psum_tensor("psum_Sb", [128, 2048], F32))
        psum_S = [psum_Sb[:, kt * 512:(kt + 1) * 512] for kt in range(4)]
        psum_ctx = [es5.enter_context(nc.psum_tensor(f"psum_ctx{i}", [128, 512], F32)) for i in range(2)]
        psum_d = es5.enter_context(nc.psum_tensor("psum_d", [128, 512], F32))
        psum_db = es5.enter_context(nc.psum_tensor("psum_db", [128, 512], F32))

        NWIN = H * 5
        WIN = lambda j: (j // 5,) + WINS[j % 5]
        bT, bC, bV = sT.v, sC.v, sV.v
        t_s = {}
        t_d = {}
        t_ctx = {}
        t_db = {}
        _tt = bT
        for i in range(NWIN + 2):
            if i < NWIN:
                _tt += 1; t_s[i] = _tt
            if 1 <= i <= NWIN:
                _tt += 1; t_d[i - 1] = _tt
                _tt += 1; t_ctx[i - 1] = _tt
            if 2 <= i <= NWIN + 1:
                _tt += 1; t_db[i - 2] = _tt
        c_sig = {}
        c_db = {}
        _cc = bC
        for i in range(NWIN + 2):
            if i < NWIN:
                _cc += 1; c_sig[i] = _cc
            if 2 <= i <= NWIN + 1:
                _cc += 1; c_db[i - 2] = _cc
        v_rd = {}
        v_dc = {}
        _vv = bV
        for i in range(NWIN + 2):
            if 1 <= i <= NWIN:
                _vv += 1; v_rd[i - 1] = _vv
            if 2 <= i <= NWIN + 1:
                _vv += 1; v_dc[i - 2] = _vv
        nblend = {-1: 0}
        _nb = 0
        for j in range(NWIN):
            if j % 5 >= 1:
                _nb += 1
            nblend[j] = _nb
        bP = sP.v

        with nc.named_scope("p5"), nc.Block("p5") as blk:
            @blk.sync
            def _(sync):
                sW.inc(sync.dma_start(out=fbC[:, :], in_=alpha[:, :]), 16)
            wAL = sW.v

            @blk.tensor
            def _(tensor):
                for i in range(NWIN + 2):
                    if i < NWIN:
                        h, ks, q0, ql = WIN(i)
                        qr = bigB[:, h * 1024 + q0:h * 1024 + q0 + ql]
                        if i >= 1:
                            tensor.wait_ge(sC.h, c_sig[i - 1])
                        for kt in range(4):
                            ins = tensor.matmul(psum_S[kt][:, 0:ql], kr_s[:, ks + kt * 128:ks + (kt + 1) * 128],
                                                qr, start=True, stop=True)
                        assert sT.inc(ins) == t_s[i]
                    if 1 <= i <= NWIN:
                        j = i - 1
                        h, ks, q0, ql = WIN(j)
                        sl = (j % 2) * 2048
                        if i == 1:
                            tensor.wait_ge(sC.h, c_sig[0])
                        if j >= 2:
                            tensor.wait_ge(sV.h, v_rd[j - 2])
                        row = (j % 2) * 32
                        for kt in range(4):
                            ins = tensor.matmul(psum_d[row:row + 1, 0:ql], ones_c[:, :],
                                                sgbuf[:, sl + kt * 512:sl + kt * 512 + ql],
                                                start=kt == 0, stop=kt == 3)
                        assert sT.inc(ins) == t_d[j]
                        if j >= 2:
                            tensor.wait_ge(sV.h, v_dc[j - 2])
                        pc = psum_ctx[j % 2]
                        for kt in range(4):
                            ins = tensor.matmul(pc[:, 0:ql], vtm_s[:, ks + kt * 128:ks + (kt + 1) * 128],
                                                sgbuf[:, sl + kt * 512:sl + kt * 512 + ql],
                                                start=kt == 0, stop=kt == 3)
                        assert sT.inc(ins) == t_ctx[j]
                    if 2 <= i <= NWIN + 1:
                        j = i - 2
                        h, ks, q0, ql = WIN(j)
                        tensor.wait_ge(sV.h, v_rd[j])
                        if j >= 1:
                            tensor.wait_ge(sC.h, c_db[j - 1])
                        rh = rd_hl[j % 2]
                        ins = tensor.matmul(psum_db[:, 0:ql], ones_r[:, :], rh[0:1, 0:ql], start=True, stop=True)
                        assert sT.inc(ins) == t_db[j]

            @blk.scalar
            def _(scalar):
                for i in range(NWIN + 2):
                    if i < NWIN:
                        h, ks, q0, ql = WIN(i)
                        sl = (i % 2) * 2048
                        if i == 0:
                            scalar.wait_ge(sW.h, wAL)
                        scalar.wait_ge(sT.h, t_s[i])
                        if i >= 2:
                            scalar.wait_ge(sT.h, t_ctx[i - 2])
                        if i % 5 in (1, 2, 3):
                            ins = scalar.activation(out=sgbuf[:, sl:sl + 2048],
                                                    in_=psum_Sb[:, :], func=AF.Sigmoid)
                        else:
                            for kt in range(4):
                                kc = ks // 128 + kt
                                ins = scalar.activation(out=sgbuf[:, sl + kt * 512:sl + kt * 512 + ql],
                                                        in_=psum_S[kt][:, 0:ql],
                                                        func=AF.Sigmoid, bias=maskk_s[:, kc:kc + 1], scale=1.0)
                        assert sC.inc(ins) == c_sig[i]
                    if 2 <= i <= NWIN + 1:
                        j = i - 2
                        h, ks, q0, ql = WIN(j)
                        scalar.wait_ge(sT.h, t_db[j])
                        if j >= 1:
                            scalar.wait_ge(sV.h, v_dc[j - 1])
                        ins = scalar.copy(out=tmp[:, 1024:1024 + ql], in_=psum_db[:, 0:ql])
                        assert sC.inc(ins) == c_db[j]

            @blk.vector
            def _(vector):
                for i in range(NWIN + 2):
                    if 1 <= i <= NWIN:
                        j = i - 1
                        h, ks, q0, ql = WIN(j)
                        row = (j % 2) * 32
                        vector.wait_ge(sT.h, t_d[j])
                        if j >= 2:
                            vector.wait_ge(sT.h, t_db[j - 2])
                        rh = rd_hl[j % 2]
                        with nc.allow_low_precision(reason="bf16 attention prob normalizer (probs are bf16)"):
                            ins = vector.reciprocal(rh[0:1, 0:ql], psum_d[row:row + 1, 0:ql])
                        assert sV.inc(ins) == v_rd[j]
                    if 2 <= i <= NWIN + 1:
                        j = i - 2
                        h, ks, q0, ql = WIN(j)
                        vector.wait_ge(sC.h, c_db[j])
                        vector.wait_ge(sT.h, t_ctx[j])
                        if j >= 2:
                            vector.wait_ge(sP.h, bP + nblend[j - 1])
                        ins = vector.tensor_mul(ctxn[j % 2][:, 0:ql], psum_ctx[j % 2][:, 0:ql],
                                                tmp[:, 1024:1024 + ql])
                        assert sV.inc(ins) == v_dc[j]

            @blk.gpsimd
            def _(gpsimd):
                first = True
                for j in range(NWIN):
                    wi = j % 5
                    if wi < 1:
                        continue
                    h = j // 5
                    if first:
                        gpsimd.wait_ge(sW.h, wAL)
                        first = False
                    gpsimd.wait_ge(sV.h, v_dc[j])
                    b = wi - 1
                    poff = 0 if b == 0 else 256
                    cur = ctxn[j % 2]
                    prev = ctxn[(j - 1) % 2]
                    gpsimd.tensor_sub(tmp2g[:, 0:256], cur[:, 0:256], prev[:, poff:poff + 256])
                    gpsimd.tensor_mul(tmp2g[:, 0:256], tmp2g[:, 0:256], fbC[:, b * 256:(b + 1) * 256])
                    ins = gpsimd.tensor_add(bigA[:, h * 1024 + b * 256:h * 1024 + (b + 1) * 256],
                                            tmp2g[:, 0:256], prev[:, poff:poff + 256])
                    assert sP.inc(ins) == bP + nblend[j]
        es5.close()

        es6s = ExitStack()
        psum_s2 = es6s.enter_context(nc.psum_tensor("psum_s2", [128, OWN], F32))
        psum_q2 = es6s.enter_context(nc.psum_tensor("psum_q2", [128, OWN], F32))
        es6 = ExitStack()
        psum_wo = [es6.enter_context(nc.psum_tensor(f"psum_wo{i}", [128, OWN], F32)) for i in range(2)]

        bT, bC, bV = sT.v, sC.v, sV.v
        bS1 = sS1.v
        t_mm6 = {}
        t_st6 = {}
        _tt = bT
        for oc in range(32):
            _tt += 1; t_mm6[oc] = _tt
            if oc >= 1:
                _tt += 1; t_st6[oc - 1] = _tt
        _tt += 1; t_st6[31] = _tt
        v_x1 = lambda oc: bV + oc + 1
        c_sb6 = lambda oc: bC + oc + 1
        bGs = [s.v for s in sG]
        g_after_p6 = []
        gtmp = list(bGs)
        for j in range(64):
            gtmp[j % 3] += 16
            g_after_p6.append(gtmp[j % 3])
        bXa, bXb = sXa.v, sXb.v
        x_after = lambda oc: (bXa if oc % 2 == 0 else bXb) + 16 * (oc // 2 + 1)

        with nc.named_scope("p6"), nc.Block("p6") as blk:
            @blk.gpsimd
            def _(gpsimd):
                for j in range(64):
                    if j >= 3:
                        gpsimd.wait_ge(sT.h, t_mm6[(j - 3) // 2])
                    slot = j % 3
                    oc, hf = j // 2, j % 2
                    ins = gpsimd.dma_start(out=wbf[:, slot * 2048:(slot + 1) * 2048],
                                           in_=wo[oc, :, hf * 2048:(hf + 1) * 2048])
                    assert sG[slot].inc(ins, 16) == g_after_p6[j]

            @blk.sync
            def _(sync):
                for oc in range(2):
                    sem = sXa if oc % 2 == 0 else sXb
                    fb = fbA if oc % 2 == 0 else fbB
                    assert sem.inc(sync.dma_start(out=fb[:, 0:1024],
                                                  in_=xT[oc * 128:(oc + 1) * 128, 256:1280]), 16) == x_after(oc)
                for oc in range(32):
                    sync.wait_ge(sV.h, v_x1(oc))
                    sS1.inc(sync.dma_start(out=x1scr[oc * 128:(oc + 1) * 128, :],
                                           in_=(fbC if oc % 2 == 0 else fbD)[:, :]), 16)
                    if oc + 2 < 32:
                        oc2 = oc + 2
                        sem = sXa if oc2 % 2 == 0 else sXb
                        fb = fbA if oc2 % 2 == 0 else fbB
                        assert sem.inc(sync.dma_start(out=fb[:, 0:1024],
                                                      in_=xT[oc2 * 128:(oc2 + 1) * 128, 256:1280]), 16) == x_after(oc2)

            @blk.tensor
            def _(tensor):
                def st_mms(j):
                    tensor.wait_ge(sC.h, c_sb6(j))
                    st_, sp_ = j == 0, j == 31
                    so = (j % 2) * 1024
                    x1b = bigB[:, j * 1024:(j + 1) * 1024]
                    tensor.matmul(psum_s2[0:1, 0:512], ones_c[:, :], x1b[:, 0:512], start=st_, stop=sp_)
                    tensor.matmul(psum_s2[0:1, 512:1024], ones_c[:, :], x1b[:, 512:1024], start=st_, stop=sp_)
                    tensor.matmul(psum_q2[0:1, 0:512], ones_c[:, :], sgbuf[:, so:so + 512], start=st_, stop=sp_)
                    ins = tensor.matmul(psum_q2[0:1, 512:1024], ones_c[:, :], sgbuf[:, so + 512:so + 1024], start=st_, stop=sp_)
                    assert sT.inc(ins) == t_st6[j]
                for oc in range(32):
                    if oc >= 2:
                        tensor.wait_ge(sV.h, v_x1(oc - 2))
                    pw = psum_wo[oc % 2]
                    for fs in range(NF):
                        if fs == 0:
                            tensor.wait_ge(sG[(2 * oc) % 3].h, g_after_p6[2 * oc])
                        if fs == 16:
                            tensor.wait_ge(sG[(2 * oc + 1) % 3].h, g_after_p6[2 * oc + 1])
                        slot = (2 * oc + fs // 16) % 3
                        lhs = wbf[:, slot * 2048 + (fs % 16) * 128:slot * 2048 + (fs % 16 + 1) * 128]
                        st_, sp_ = fs == 0, fs == NF - 1
                        tensor.matmul(pw[:, 0:512], lhs, bigA[:, fs * 1024:fs * 1024 + 512], start=st_, stop=sp_)
                        ins = tensor.matmul(pw[:, 512:1024], lhs, bigA[:, fs * 1024 + 512:fs * 1024 + 1024], start=st_, stop=sp_)
                    assert sT.inc(ins) == t_mm6[oc]
                    if oc >= 1:
                        st_mms(oc - 1)
                st_mms(31)

            @blk.vector
            def _(vector):
                for oc in range(32):
                    vector.wait_ge(sT.h, t_mm6[oc])
                    vector.wait_ge((sXa if oc % 2 == 0 else sXb).h, x_after(oc))
                    if oc >= 2:
                        vector.wait_ge(sS1.h, bS1 + 16 * (oc - 1))
                        vector.wait_ge(sC.h, c_sb6(oc - 2))
                    fb = fbA if oc % 2 == 0 else fbB
                    x1f = fbC if oc % 2 == 0 else fbD
                    ins = vector.tensor_add(x1f[:, :], psum_wo[oc % 2][:, :], fb[:, 0:1024])
                    assert sV.inc(ins) == v_x1(oc)

            @blk.scalar
            def _(scalar):
                for oc in range(32):
                    scalar.wait_ge(sV.h, v_x1(oc))
                    if oc >= 2:
                        scalar.wait_ge(sT.h, t_st6[oc - 2])
                    x1f = fbC if oc % 2 == 0 else fbD
                    so = (oc % 2) * 1024
                    scalar.copy(out=bigB[:, oc * 1024:(oc + 1) * 1024], in_=x1f[:, :])
                    ins = scalar.activation(out=sgbuf[:, so:so + 1024], in_=x1f[:, :], func=AF.Square)
                    assert sC.inc(ins) == c_sb6(oc)
        es6.close()

        m2 = fbA[0:1, 0:OWN]
        w2 = fbB[0:1, 0:OWN]
        hi2 = sgbuf[0:1, 0:OWN]
        lo2 = sgbuf[0:1, 1024:2048]
        ln_coef_block("p7a", psum_s2, psum_q2, OWN, m2, w2, hi2, lo2)
        es6s.close()

        bcast_block("p7b", OWN, m2, hi2, lo2)

        es7c = ExitStack()
        psum_r = es7c.enter_context(nc.psum_tensor("psum_r", [8, 1024], F32))
        psum_tr = es7c.enter_context(nc.psum_tensor("psum_tr", [128, 8], F32))
        psum_tw = es7c.enter_context(nc.psum_tensor("psum_tw", [8, 128], BF16))
        psum_we = es7c.enter_context(nc.psum_tensor("psum_we", [128, 1024], F32))
        psum_h = es7c.enter_context(nc.psum_tensor("psum_h", [128, 1024], F32))

        bT, bC, bV, bP7 = sT.v, sC.v, sV.v, sP.v
        vnorm7 = {}
        gnorm7 = {}
        _vv = bV
        for fs in range(NF):
            if fs not in GPSET:
                _vv += 1; vnorm7[fs] = _vv
        nvn7 = _vv
        _gg = bP7
        for fs in sorted(GPSET):
            _gg += 1; gnorm7[fs] = _gg
        t_r = bT + 1
        t_tr = lambda qt: bT + 2 + qt
        t_tw = lambda qt: bT + 10 + qt
        t_we = bT + 18
        t_h = bT + 19
        c_lg = bC + 1
        c_z = lambda qt: bC + 2 + qt
        c_wsb = lambda qt: bC + 10 + qt
        c_we = bC + 18
        v_w = lambda qt: nvn7 + qt + 1

        wALL7 = sW.v

        with nc.named_scope("p7c"), nc.Block("p7c") as blk:
            @blk.gpsimd
            def _(gpsimd):
                sG[0].inc(gpsimd.dma_start(out=wbf[:, 0:4096], in_=apk[:, :]), 16)
                for fs in sorted(GPSET):
                    a0 = fs * 1024
                    for hh in range(2):
                        o = a0 + hh * 512
                        gpsimd.tensor_mul(tmp2g[:, :], bigB[:, o:o + 512], a_sb[:, hh * 512:(hh + 1) * 512])
                        ins = gpsimd.tensor_add(bigB[:, o:o + 512], tmp2g[:, :], b_sb[:, hh * 512:(hh + 1) * 512])
                    assert sP.inc(ins) == gnorm7[fs]
            gAP = sG[0].v

            @blk.vector
            def _(vector):
                for fs in range(NF):
                    if fs in GPSET:
                        continue
                    a0 = fs * 1024
                    vector.tensor_mul(tmp[:, 0:1024], bigB[:, a0:a0 + 1024], a_sb[:, :])
                    ins = vector.tensor_add(bigB[:, a0:a0 + 1024], tmp[:, 0:1024], b_sb[:, :])
                    assert sV.inc(ins) == vnorm7[fs]

            @blk.tensor
            def _(tensor):
                tensor.wait_ge(sW.h, wALL7)
                for fs in range(NF):
                    st_, sp_ = fs == 0, fs == NF - 1
                    if fs in GPSET:
                        tensor.wait_ge(sP.h, gnorm7[fs])
                    else:
                        tensor.wait_ge(sV.h, vnorm7[fs])
                    lhs = wr_b[:, fs * 8:(fs + 1) * 8]
                    tensor.matmul(psum_r[0:8, 0:512], lhs, bigB[:, fs * 1024:fs * 1024 + 512], start=st_, stop=sp_)
                    ins = tensor.matmul(psum_r[0:8, 512:1024], lhs, bigB[:, fs * 1024 + 512:fs * 1024 + 1024], start=st_, stop=sp_)
                assert sT.inc(ins) == t_r
                tensor.wait_ge(sC.h, c_lg)
                for qt in range(8):
                    if qt >= 1:
                        tensor.wait_ge(sC.h, c_z(qt - 1))
                    ins = tensor.transpose(psum_tr[:, :], lg_sb[0:8, qt * 128:(qt + 1) * 128], idf8_s[:, :])
                    assert sT.inc(ins) == t_tr(qt)
                for qt in range(8):
                    tensor.wait_ge(sV.h, v_w(qt))
                    if qt >= 1:
                        tensor.wait_ge(sC.h, c_wsb(qt - 1))
                    ins = tensor.transpose(psum_tw[:, :], w_tm[:, qt * 8:(qt + 1) * 8], identb_s[:, :])
                    assert sT.inc(ins) == t_tw(qt)
                tensor.wait_ge(sC.h, c_wsb(7))
                tensor.matmul(psum_we[:, 0:512], eexp_s[:, :], w_sb[0:8, 0:512], start=True, stop=True)
                ins = tensor.matmul(psum_we[:, 512:1024], eexp_s[:, :], w_sb[0:8, 512:1024], start=True, stop=True)
                assert sT.inc(ins) == t_we
                tensor.wait_ge(sG[0].h, gAP)
                for fs in range(NF):
                    st_, sp_ = fs == 0, fs == NF - 1
                    lhs = wbf[:, fs * 128:(fs + 1) * 128]
                    tensor.matmul(psum_h[:, 0:512], lhs, bigB[:, fs * 1024:fs * 1024 + 512], start=st_, stop=sp_)
                    ins = tensor.matmul(psum_h[:, 512:1024], lhs, bigB[:, fs * 1024 + 512:fs * 1024 + 1024], start=st_, stop=sp_)
                assert sT.inc(ins) == t_h

            @blk.scalar
            def _(scalar):
                scalar.wait_ge(sT.h, t_r)
                ins = scalar.activation(out=lg_sb[0:8, 0:1024], in_=psum_r[0:8, :], func=AF.Identity, bias=br_s[0:8, 0:1])
                assert sC.inc(ins) == c_lg
                for qt in range(8):
                    scalar.wait_ge(sT.h, t_tr(qt))
                    ins = scalar.copy(out=z_sb[:, qt * 8:(qt + 1) * 8], in_=psum_tr[:, :])
                    assert sC.inc(ins) == c_z(qt)
                for qt in range(8):
                    scalar.wait_ge(sT.h, t_tw(qt))
                    ins = scalar.copy(out=w_sb[0:8, qt * 128:(qt + 1) * 128], in_=psum_tw[:, :])
                    assert sC.inc(ins) == c_wsb(qt)
                scalar.wait_ge(sT.h, t_we)
                ins = scalar.copy(out=tmp[:, 0:1024], in_=psum_we[:, :])
                assert sC.inc(ins) == c_we

            @blk.vector
            def _(vector):
                for qt in range(8):
                    vector.wait_ge(sC.h, c_z(qt))
                    z = z_sb[:, qt * 8:(qt + 1) * 8]
                    vector.memset(mk_sb[:, :], 1.0)
                    for it in range(8):
                        vector.tensor_mul(z8[:, :], z, mk_sb[:, :])
                        vector.reduce_sum(col_s[:, :], z8[:, :], axis=AX)
                        vector.reduce_sum(col_c[:, :], mk_sb[:, :], axis=AX)
                        vector.tensor_scalar_add(col_s[:, :], col_s[:, :], -1.0)
                        vector.reciprocal(col_c[:, :], col_c[:, :])
                        vector.tensor_mul(col_t[:, :], col_s[:, :], col_c[:, :])
                        if it < 7:
                            vector.tensor_single_scalar(mk_sb[:, :], z, col_t[:, :], OP.is_gt)
                    ins = vector.tensor_scalar(out=w_tm[:, qt * 8:(qt + 1) * 8], in0=z,
                                               scalar1=col_t[:, :], scalar2=0.0,
                                               op0=OP.subtract, op1=OP.max)
                    assert sV.inc(ins) == v_w(qt)
                vector.wait_ge(sT.h, t_h)
                vector.wait_ge(sC.h, c_we)
                sV.inc(vector.tensor_mul(u_bf[:, :], psum_h[:, :], tmp[:, 0:1024]))
        es7c.close()

        es8 = ExitStack()
        psum_o = [es8.enter_context(nc.psum_tensor(f"psum_o{i}", [128, OWN], F32)) for i in range(2)]

        bT, bV = sT.v, sV.v
        bS1_tot, bS2 = sS1.v, sS2.v
        t_mm8 = lambda oc: bT + oc + 1
        v_out = lambda oc: bV + oc + 1
        bGs = [s.v for s in sG]
        bG2 = sG[2].v
        g_after_p8 = []
        gtmp = [bGs[0], bGs[1], bGs[2] + 16]
        for j in range(64):
            gtmp[j % 3] += 16
            g_after_p8.append(gtmp[j % 3])
        bXa, bXb = sXa.v, sXb.v
        x_after8 = lambda oc: (bXa if oc % 2 == 0 else bXb) + 16 * (oc // 2 + 1)

        wALL8 = sW.v

        with nc.named_scope("p8"), nc.Block("p8") as blk:
            @blk.gpsimd
            def _(gpsimd):
                sG[2].inc(gpsimd.dma_start(out=sgbuf[:, :], in_=bpk[:, :]), 16)
                for j in range(64):
                    if j >= 3:
                        gpsimd.wait_ge(sT.h, t_mm8((j - 3) // 2))
                    slot = j % 3
                    oc, hf = j // 2, j % 2
                    ins = gpsimd.dma_start(out=wbf[:, slot * 2048:(slot + 1) * 2048],
                                           in_=wb[oc, :, hf * 2048:(hf + 1) * 2048])
                    assert sG[slot].inc(ins, 16) == g_after_p8[j]

            @blk.sync
            def _(sync):
                sync.wait_ge(sS1.h, bS1_tot)
                for oc in range(2):
                    sem = sXa if oc % 2 == 0 else sXb
                    fb = fbA if oc % 2 == 0 else fbB
                    assert sem.inc(sync.dma_start(out=fb[:, 0:1024],
                                                  in_=x1scr[oc * 128:(oc + 1) * 128, :]), 16) == x_after8(oc)
                for oc in range(32):
                    sync.wait_ge(sV.h, v_out(oc))
                    sS2.inc(sync.dma_start(out=outT[oc * 128:(oc + 1) * 128, :],
                                           in_=(fbC if oc % 2 == 0 else fbD)[:, :]), 16)
                    if oc + 2 < 32:
                        oc2 = oc + 2
                        sem = sXa if oc2 % 2 == 0 else sXb
                        fb = fbA if oc2 % 2 == 0 else fbB
                        assert sem.inc(sync.dma_start(out=fb[:, 0:1024],
                                                      in_=x1scr[oc2 * 128:(oc2 + 1) * 128, :]), 16) == x_after8(oc2)
                sync.wait_ge(sS2.h, bS2 + 16 * 32)

            @blk.tensor
            def _(tensor):
                tensor.wait_ge(sG[2].h, bG2 + 16)
                for oc in range(32):
                    if oc >= 2:
                        tensor.wait_ge(sV.h, v_out(oc - 2))
                    po = psum_o[oc % 2]
                    for fs in range(NF):
                        if fs == 0:
                            tensor.wait_ge(sG[(2 * oc) % 3].h, g_after_p8[2 * oc])
                        if fs == 16:
                            tensor.wait_ge(sG[(2 * oc + 1) % 3].h, g_after_p8[2 * oc + 1])
                        slot = (2 * oc + fs // 16) % 3
                        lhs = wbf[:, slot * 2048 + (fs % 16) * 128:slot * 2048 + (fs % 16 + 1) * 128]
                        st_ = fs == 0
                        tensor.matmul(po[:, 0:512], lhs, bigB[:, fs * 1024:fs * 1024 + 512], start=st_, stop=False)
                        tensor.matmul(po[:, 512:1024], lhs, bigB[:, fs * 1024 + 512:fs * 1024 + 1024], start=st_, stop=False)
                    blh = sgbuf[:, oc * 128:(oc + 1) * 128]
                    tensor.matmul(po[:, 0:512], blh, u_bf[:, 0:512], start=False, stop=True)
                    ins = tensor.matmul(po[:, 512:1024], blh, u_bf[:, 512:1024], start=False, stop=True)
                    assert sT.inc(ins) == t_mm8(oc)

            @blk.vector
            def _(vector):
                for oc in range(32):
                    if oc == 0:
                        vector.wait_ge(sW.h, wALL8)
                    vector.wait_ge(sT.h, t_mm8(oc))
                    vector.wait_ge((sXa if oc % 2 == 0 else sXb).h, x_after8(oc))
                    if oc >= 2:
                        vector.wait_ge(sS2.h, bS2 + 16 * (oc - 1))
                    fb = fbA if oc % 2 == 0 else fbB
                    outf = fbC if oc % 2 == 0 else fbD
                    ins = vector.scalar_tensor_tensor(outf[:, :], psum_o[oc % 2][:, :],
                                                      bb_s[:, oc:oc + 1], fb[:, 0:1024],
                                                      OP.add, OP.add)
                    assert sV.inc(ins) == v_out(oc)
        es8.close()

    return nc


def _rope_tables(dt):
    inv = 1.0 / (10000.0 ** (np.arange(0, DH, 2, dtype=np.float64) / DH))
    t = np.arange(S, dtype=np.float64)
    f = np.outer(t, inv)
    cos = np.zeros((S, DH), np.float64)
    sin = np.zeros((S, DH), np.float64)
    cos[:, 0::2] = np.cos(f); cos[:, 1::2] = np.cos(f)
    sin[:, 0::2] = np.sin(f); sin[:, 1::2] = np.sin(f)
    return cos.astype(dt), sin.astype(dt)


def kernel(L0_attn_x, L0_attn_Wq, L0_attn_Wk, L0_attn_Wv, L0_attn_Wo,
           L0_ffn_W_base, L0_ffn_b_base, L0_ffn_W_router, L0_ffn_b_router,
           L0_ffn_lora_A, L0_ffn_lora_B):
    x = np.asarray(L0_attn_x)
    dt = x.dtype
    bf = ml_dtypes.bfloat16

    def lay_big(W):
        return np.ascontiguousarray(
            np.asarray(W).reshape(32, 128, 32, 128).transpose(0, 3, 2, 1).reshape(32, 128, 4096))

    wq_l = lay_big(L0_attn_Wq)
    wo_l = lay_big(L0_attn_Wo)
    wb_l = lay_big(L0_ffn_W_base)
    wk_l = np.ascontiguousarray(np.asarray(L0_attn_Wk).reshape(128, 32, 128).transpose(2, 1, 0).reshape(128, 4096))
    wv_l = np.ascontiguousarray(np.asarray(L0_attn_Wv).reshape(128, 32, 128).transpose(2, 1, 0).reshape(128, 4096))
    wr_l = np.ascontiguousarray(np.asarray(L0_ffn_W_router).reshape(8, 32, 128).transpose(2, 1, 0).reshape(128, 256))
    apk_l = np.ascontiguousarray(np.asarray(L0_ffn_lora_A).transpose(1, 0, 2).reshape(4096, 128)
                                 .reshape(32, 128, 128).transpose(1, 0, 2).reshape(128, 4096))
    bpk_l = np.ascontiguousarray(np.asarray(L0_ffn_lora_B).reshape(128, 4096))
    bb_l = np.ascontiguousarray(np.asarray(L0_ffn_b_base).reshape(32, 128).T)
    br_l = np.ascontiguousarray(np.asarray(L0_ffn_b_router).reshape(8, 1))

    cosS, sinS = _rope_tables(dt)
    SCALE = dt.type(1.0) / np.sqrt(dt.type(DH))

    identb = np.eye(128).astype(bf)
    idf8 = np.eye(8).astype(dt)
    P = np.zeros((128, 128), np.float64)
    for i in range(64):
        P[i, i + 64] = -1.0
        P[i + 64, i] = 1.0
    ptr_l = np.ascontiguousarray(P.T).astype(bf)
    eexp_l = np.zeros((8, 128), np.float64)
    for e in range(8):
        eexp_l[e, e * 16:(e + 1) * 16] = 1.0
    eexp_l = eexp_l.astype(bf)

    xT_full = np.ascontiguousarray(x.reshape(S, D).T)
    base_alpha = np.tile(np.linspace(0.0, 1.0, 256, dtype=dt), 4)

    in_maps = []
    for c in range(NC):
        lo, hi = c * OWN, (c + 1) * OWN
        xTc = np.zeros((D, HALO), dt)
        s0, s1 = max(lo - 256, 0), min(hi + 256, S)
        xTc[:, s0 - (lo - 256):s1 - (lo - 256)] = xT_full[:, s0:s1]
        mask = np.full(HALO, -40.0, np.float64)
        mask[s0 - (lo - 256):s1 - (lo - 256)] = 0.0
        maskk_c = np.ascontiguousarray(mask.reshape(12, 128).T).astype(dt)
        al = base_alpha.copy()
        if c == 0:
            al[:256] = 1.0
        alpha_c = np.ascontiguousarray(np.broadcast_to(al, (128, OWN)))
        pos_k = np.clip(np.arange(lo - 256, hi + 256), 0, S - 1)
        in_maps.append({
            "xT": xTc,
            "wq": wq_l, "wo": wo_l, "wb": wb_l, "wk": wk_l, "wv": wv_l,
            "wr": wr_l, "apk": apk_l, "bpk": bpk_l, "bb": bb_l, "br": br_l,
            "cosq": np.ascontiguousarray((cosS[lo:hi] * SCALE).T),
            "sinq": np.ascontiguousarray((sinS[lo:hi] * SCALE).T),
            "cosk": np.ascontiguousarray(cosS[pos_k].T),
            "sink": np.ascontiguousarray(sinS[pos_k].T),
            "maskk": maskk_c, "alpha": alpha_c,
            "identb": identb, "idf8": idf8, "ptr": ptr_l, "eexp": eexp_l,
        })

    nc = build_nc()
    res = run_bass_kernel_spmd(nc, in_maps, core_ids=list(range(NC)))
    outs = [res.results[c]["outT"] for c in range(NC)]
    full = np.concatenate(outs, axis=1)
    return np.ascontiguousarray(full.T).reshape(1, S, D)
